# version 25
# baseline (speedup 1.0000x reference)
import numpy as np
from contextlib import ExitStack

import concourse.bass as bass
import concourse.mybir as mybir
from concourse.bass_utils import run_bass_kernel_spmd

F32 = mybir.dt.float32
BF16 = mybir.dt.bfloat16
AF = mybir.ActivationFunctionType
ALU = mybir.AluOpType
AX = mybir.AxisListType

EPS = 1e-5
LN8 = float(np.log(8.0))

ENGS = ("sync", "scalar", "vector", "tensor", "gpsimd")


class Sem:
    def __init__(self, h):
        self.h = h
        self.n = 0


class Emit:

    def __init__(self):
        self.q = {k: [] for k in ENGS}

    def op(self, eng, fn, inc=None, k=1):
        if inc is not None:
            self.q[eng].append(lambda e, fn=fn, s=inc.h, k=k: fn(e).then_inc(s, k))
            inc.n += k
        else:
            self.q[eng].append(fn)

    def wait(self, eng, sem, val=None):
        v = sem.n if val is None else val
        if v > 0:
            self.q[eng].append(lambda e, s=sem.h, v=v: e.wait_ge(s, v))

    def drain(self, eng):
        self.q[eng].append(lambda e: e.drain())

    def replay(self, block):
        for eng in ENGS:
            ops = self.q[eng]
            if not ops:
                continue
            getattr(block, eng)(lambda b, ops=ops: [f(b) for f in ops])


def build_nc(S=4096, D=1024, H=16, dh=64):
    CT = D // 128
    NT = S // 128
    QC = S // 512
    KT = S // 128
    NP = H // 2
    EC = D // 512
    HW = H * 65

    nc = bass.Bass("TRN2", target_bir_lowering=False, debug=False)

    x_h = nc.dram_tensor("x", [S, D], F32, kind="ExternalInput").ap()
    w_hbms = {n: nc.dram_tensor(n, [D, D], F32, kind="ExternalInput").ap()
              for n in ("wq", "wk", "wv", "wo", "wrt", "wkt", "wvt")}
    gam_h = nc.dram_tensor("gamma", [128, D // 128], F32, kind="ExternalInput").ap()
    bet_h = nc.dram_tensor("beta", [128, D // 128], F32, kind="ExternalInput").ap()
    mur_h = nc.dram_tensor("mur", [128, CT], F32, kind="ExternalInput").ap()
    muk_h = nc.dram_tensor("muk", [128, CT], F32, kind="ExternalInput").ap()
    out_h = nc.dram_tensor("out", [S, D], F32, kind="ExternalOutput").ap()

    xa_hbm = nc.dram_tensor("xa_h", [S, D], BF16, kind="Internal").ap()
    qt_hbm = nc.dram_tensor("qt_h", [NP * 128, S], BF16, kind="Internal").ap()
    attn_hbm = nc.dram_tensor("attn_h", [D, S], BF16, kind="Internal").ap()
    x1_hbm = nc.dram_tensor("x1_h", [S, D], F32, kind="Internal").ap()
    xn_hbm = nc.dram_tensor("xn_h", [S + 16, D], BF16, kind="Internal").ap()

    with ExitStack() as top:
        E = top.enter_context

        def sem(name):
            return Sem(E(nc.semaphore(name)))

        def sem2(name):
            return [Sem(E(nc.semaphore(name + "0"))), Sem(E(nc.semaphore(name + "1")))]

        def sem4(name):
            return [Sem(E(nc.semaphore(name + str(i)))) for i in range(4)]

        d_x = sem4("d_x")
        d_wc = sem("d_wc")
        d_w1 = sem("d_w1")
        d_wv = sem("d_wv")
        d_wo = sem("d_wo")
        d_wf = sem("d_wf")
        d_xa = sem4("d_xa")
        d_xat = sem2("d_xat")
        d_qts = sem4("d_qts")
        d_qtr = sem2("d_qtr")
        d_attn = sem2("d_attn")
        d_at3 = sem2("d_at3")
        d_x3 = sem2("d_x3")
        d_x1w = sem4("d_x1w")
        d_xnw = sem2("d_xnw")
        d_tr3 = sem2("d_tr3")
        d_x1r = sem2("d_x1r")
        d_out = sem2("d_out")

        s_v = sem("s_v")
        s_s = sem("s_s")
        s_t = sem("s_t")
        s_sct = sem("s_sct")
        s_exp = sem("s_exp")
        s_mm2 = sem("s_mm2")
        s_aev = sem("s_aev")
        s_rec = sem("s_rec")
        s_bct = sem("s_bct")
        s_amu = sem("s_amu")
        s_pjt = sem("s_pjt")
        s_pjw = sem("s_pjw")
        s_pjv = sem("s_pjv")
        s_gm = sem("s_gm")

        block = E(nc.Block())

        perm = E(ExitStack())
        P = perm.enter_context
        ones2f = P(nc.sbuf_tensor("ones2f", [2, 128], F32))
        ones1b = P(nc.sbuf_tensor("ones1b", [1, 128], BF16))
        ones128b = P(nc.sbuf_tensor("ones128b", [128, 1], BF16))
        ones128f = P(nc.sbuf_tensor("ones128f", [128, 64], F32))
        bias_cols = P(nc.sbuf_tensor("bias_cols", [128, NP], F32))
        mur_sb = P(nc.sbuf_tensor("mur_sb", [128, CT], F32))
        muk_sb = P(nc.sbuf_tensor("muk_sb", [128, CT], F32))
        qm_sb = P(nc.sbuf_tensor("qm_sb", [1, 2 * NP * QC], F32))
        qkm_sb = P(nc.sbuf_tensor("qkm_sb", [1, 2 * NP], F32))
        prod_sb = P(nc.sbuf_tensor("prod_sb", [1, NP], F32))
        nmag_sb = P(nc.sbuf_tensor("nmag_sb", [1, NP], F32))
        st_sb = P(nc.sbuf_tensor("st_sb", [128, 12], F32))
        mv_sb = [P(nc.sbuf_tensor(f"mv{i}", [128, 2], F32)) for i in range(4)]
        sc_sb = [P(nc.sbuf_tensor(f"sc{i}", [128, 1], F32)) for i in range(4)]
        ms_sb = P(nc.sbuf_tensor("ms_sb", [128, 4], F32))
        nms_sb = P(nc.sbuf_tensor("nms_sb", [128, 4], F32))
        gcol_sb = P(nc.sbuf_tensor("gcol_sb", [128, CT], F32))
        bcol_sb = P(nc.sbuf_tensor("bcol_sb", [128, CT], F32))
        bcolb_sb = P(nc.sbuf_tensor("bcolb_sb", [128, CT], BF16))
        ceps2_sb = P(nc.sbuf_tensor("ceps2_sb", [128, 1], F32))
        ceps_sb = P(nc.sbuf_tensor("ceps_sb", [128, 1], F32))
        cln8_sb = P(nc.sbuf_tensor("cln8_sb", [1, 1], F32))
        zcol_sb = P(nc.sbuf_tensor("zcol_sb", [128, 1], BF16))
        ch05_sb = P(nc.sbuf_tensor("ch05_sb", [128, 1], F32))

        ph = Emit()
        ph.op("vector", lambda e: e.memset(ones2f[:], 1.0))
        ph.op("vector", lambda e: e.memset(ones1b[:], 1.0))
        ph.op("vector", lambda e: e.memset(ones128b[:], 1.0))
        ph.op("vector", lambda e: e.memset(ones128f[:], 1.0))
        ph.op("vector", lambda e: e.memset(ceps2_sb[:], EPS * EPS))
        ph.op("vector", lambda e: e.memset(ceps_sb[:], EPS))
        ph.op("vector", lambda e: e.memset(zcol_sb[:], 0.0))
        ph.op("vector", lambda e: e.memset(ch05_sb[:], 0.5))
        ph.op("vector", lambda e: e.memset(cln8_sb[:], -LN8), inc=s_v)
        ph.drain("vector")
        ph.op("gpsimd", lambda e: e.dma_start(out=mur_sb[:], in_=mur_h), inc=d_wc, k=16)
        ph.op("gpsimd", lambda e: e.dma_start(out=muk_sb[:], in_=muk_h), inc=d_wc, k=16)

        p2core = E(ExitStack())
        Pc = p2core.enter_context
        kT_sb = Pc(nc.sbuf_tensor("kT_sb", [128, NP * S], BF16))
        ps_sc = Pc(nc.psum_tensor("ps_sc", [128, 2048], F32))
        ps_o = [Pc(nc.psum_tensor(f"ps_o{i}", [128, 512], F32)) for i in range(2)]
        ps_bcj = Pc(nc.psum_tensor("ps_bcj", [128, 512], F32))
        ps_bc = [ps_bcj[0:64, :], ps_bcj[64:128, :]]
        ps_w3 = Pc(nc.psum_tensor("ps_w3", [128, 512], F32))

        r_xat = ExitStack()
        Rx = r_xat.enter_context
        xat_sb = Rx(nc.sbuf_tensor("xat_sb", [128, 2 * CT * 512], BF16, side="right"))
        sq4_sb = Rx(nc.sbuf_tensor("sq4_sb", [128, 4 * 512], BF16, side="right"))
        r_w1 = ExitStack()
        Rw = r_w1.enter_context
        wq_sb = Rw(nc.sbuf_tensor("wq_sb", [128, CT * D], BF16, side="right"))
        wk_sb = Rw(nc.sbuf_tensor("wk_sb", [128, CT * D], BF16, side="right"))
        qstage_sb = [Rw(nc.sbuf_tensor(f"qstage{i}", [128, 512], BF16, side="right")) for i in range(4)]
        bqk_sb = Rw(nc.sbuf_tensor("bqk_sb", [128, 2 * NP], F32, side="right"))
        wv_sb = Rw(nc.sbuf_tensor("wv_sb", [128, CT * D], BF16, side="right"))
        bvr_sb = Rw(nc.sbuf_tensor("bvr_sb", [1, D], BF16, side="right"))
        bvb_sb = Rw(nc.sbuf_tensor("bvb_sb", [128, D], BF16, side="right"))
        r_p1 = ExitStack()
        Rp = r_p1.enter_context
        x_sb = Rp(nc.sbuf_tensor("x_sb", [128, 4 * D], F32, side="right"))
        xa_sb = Rp(nc.sbuf_tensor("xa_sb", [128, 2 * D], BF16, side="right"))

        ph.op("gpsimd", lambda e: e.dma_start(out=gcol_sb[:], in_=gam_h), inc=d_wc, k=16)
        ph.op("gpsimd", lambda e: e.dma_start(out=bcol_sb[:], in_=bet_h), inc=d_wc, k=16)
        consts_loaded = d_wc.n
        for wn, w_sb in (("wq", wq_sb), ("wk", wk_sb)):
            for c in range(CT):
                ph.op("gpsimd", lambda e, o=w_sb[:, c * D:(c + 1) * D],
                      a=w_hbms[wn][c * 128:(c + 1) * 128, :]:
                      e.dma_start(out=o, in_=a), inc=d_w1, k=16)
        w1_loaded = d_w1.n
        for c in range(CT):
            ph.op("gpsimd", lambda e, o=wv_sb[:, c * D:(c + 1) * D],
                  a=w_hbms["wv"][c * 128:(c + 1) * 128, :]:
                  e.dma_start(out=o, in_=a), inc=d_wv, k=16)
        wv_loaded = d_wv.n

        apply_chk = []
        xa_w_chk = []
        fold1 = {}

        def emit_fold1():
            ph.wait("vector", d_wc, consts_loaded)
            ph.op("vector", lambda e: e.tensor_copy(bcolb_sb[:], bcol_sb[:]), inc=s_v)
            bcb_chk = s_v.n
            ph.wait("tensor", s_v, bcb_chk)
            ph.wait("tensor", d_w1, w1_loaded)
            ph.wait("tensor", d_wc, consts_loaded)
            cp_chk = 0
            for wi, wsb in enumerate((wq_sb, wk_sb)):
                for o in range(NP):
                    if cp_chk:
                        ph.wait("tensor", s_v, cp_chk)
                    for c in range(CT):
                        ph.op("tensor", lambda e, w=wsb, c=c, o=o,
                              st=(c == 0), sp=(c == CT - 1):
                              e.matmul(ps_o[0][:, 0:1],
                                       w[:, c * D + o * 128: c * D + o * 128 + 128],
                                       bcolb_sb[:, c:c + 1], start=st, stop=sp),
                              inc=(s_t if c == CT - 1 else None))
                    ph.wait("vector", s_t)
                    ph.op("vector", lambda e, col=wi * NP + o:
                          e.tensor_copy(bqk_sb[:, col:col + 1], ps_o[0][:, 0:1]), inc=s_v)
                    cp_chk = s_v.n
            fold1["cp_vec"] = cp_chk
            fold1["bias_mm_t"] = s_t.n
            ph.wait("vector", s_t, fold1["bias_mm_t"])
            ph.wait("vector", d_w1, w1_loaded)
            ph.wait("vector", d_wc, consts_loaded)
            for wsb in (wq_sb, wk_sb):
                for c in range(CT):
                    ph.op("vector", lambda e, w=wsb, c=c:
                          e.tensor_scalar_mul(w[:, c * D:(c + 1) * D], w[:, c * D:(c + 1) * D],
                                              gcol_sb[:, c:c + 1]),
                          inc=(s_v if (wsb is wk_sb and c == CT - 1) else None))
            fold1["done"] = s_v.n

        for i in range(NT):
            p = i % 4
            xs = x_sb[:, p * D:(p + 1) * D]
            xas = xa_sb[:, (i % 2) * D:(i % 2 + 1) * D]
            if i >= 4:
                ph.wait("sync", s_s, apply_chk[i - 4])
            ph.op("sync", lambda e, o=xs, i=i: e.dma_start(out=o, in_=x_h[i * 128:(i + 1) * 128, :]),
                  inc=d_x[p], k=16)
            x_chk = d_x[p].n
            ph.wait("vector", d_x[p], x_chk)
            if i >= 4:
                ph.wait("vector", s_s, apply_chk[i - 4])
            for half in range(EC):
                ph.op("vector", lambda e, half=half, xs=xs:
                      e.bn_stats(st_sb[:, half * 6:half * 6 + 6], xs[:, half * 512:(half + 1) * 512]))
            ph.drain("vector")
            ph.op("vector", lambda e, p=p: e.bn_aggr(mv_sb[p][:], st_sb[:, 0:6 * EC]), inc=s_v)
            ph.drain("vector")
            ph.wait("scalar", s_v)
            ph.op("scalar", lambda e, p=p:
                  e.activation(sc_sb[p][:], mv_sb[p][:, 1:2], AF.Ln, scale=(1.0 + EPS),
                               bias=ceps2_sb[:]))
            ph.drain("scalar")
            ph.op("scalar", lambda e, p=p:
                  e.activation(sc_sb[p][:], sc_sb[p][:], AF.Exp, scale=-0.5))
            ph.drain("scalar")
            ph.op("scalar", lambda e, p=p:
                  e.activation(ms_sb[:, p:p + 1], mv_sb[p][:, 0:1], AF.Copy, scale=sc_sb[p][:]))
            ph.drain("scalar")
            ph.op("scalar", lambda e, p=p:
                  e.activation(nms_sb[:, p:p + 1], ms_sb[:, p:p + 1], AF.Copy, scale=-1.0))
            ph.drain("scalar")
            if i >= 2:
                ph.wait("scalar", d_xa[(i - 2) % 4], xa_w_chk[i - 2])
            ph.op("scalar", lambda e, xas=xas, xs=xs, p=p:
                  e.activation(xas, xs, AF.Identity, scale=sc_sb[p][:], bias=nms_sb[:, p:p + 1]),
                  inc=s_s)
            apply_chk.append(s_s.n)
            ph.wait("scalar", s_s, apply_chk[i])
            ph.op("scalar", lambda e, xas=xas, i=i:
                  e.dma_start(out=xa_hbm[i * 128:(i + 1) * 128, :], in_=xas), inc=d_xa[i % 4], k=16)
            xa_w_chk.append(d_xa[i % 4].n)
            if i == 8:
                emit_fold1()
        if NT <= 8:
            emit_fold1()
        p1_dxa = tuple(s.n for s in d_xa)
        p1_ss = s_s.n

        ph.wait("tensor", d_wv, wv_loaded)
        ph.wait("tensor", d_wc, consts_loaded)
        ph.wait("tensor", s_v, fold1.get("cp_vec", 0))
        bvcp = 0
        for ec in range(EC):
            if bvcp:
                ph.wait("tensor", s_v, bvcp)
            for c in range(CT):
                ph.op("tensor", lambda e, c=c, ec=ec, st=(c == 0), sp=(c == CT - 1):
                      e.matmul(ps_o[0][0:1, 0:512], bcolb_sb[:, c:c + 1],
                               wv_sb[:, c * D + ec * 512: c * D + ec * 512 + 512],
                               start=st, stop=sp),
                      inc=(s_t if c == CT - 1 else None))
            ph.wait("vector", s_t)
            ph.op("vector", lambda e, ec=ec:
                  e.tensor_copy(bvr_sb[0:1, ec * 512:(ec + 1) * 512], ps_o[0][0:1, 0:512]),
                  inc=s_v)
            bvcp = s_v.n
        fold2_mm_t = s_t.n
        for ec in range(EC):
            ph.wait("tensor", s_v, bvcp)
            ph.op("tensor", lambda e, ec=ec:
                  e.matmul(ps_o[0][:, 0:512], ones1b[:],
                           bvr_sb[0:1, ec * 512:(ec + 1) * 512], start=True, stop=True),
                  inc=s_t)
            ph.wait("vector", s_t)
            ph.op("vector", lambda e, ec=ec:
                  e.tensor_copy(bvb_sb[:, ec * 512:(ec + 1) * 512], ps_o[0][:, 0:512]),
                  inc=s_v)
            bvcp = s_v.n
        ph.wait("vector", s_t, fold2_mm_t)
        for c in range(CT):
            ph.op("vector", lambda e, c=c:
                  e.tensor_scalar_mul(wv_sb[:, c * D:(c + 1) * D], wv_sb[:, c * D:(c + 1) * D],
                                      gcol_sb[:, c:c + 1]),
                  inc=(s_v if c == CT - 1 else None))
        fold2_done = s_v.n

        r_p1.close()
        v_sb = Pc(nc.sbuf_tensor("v_sb", [128, NT * HW], BF16))
        ph.wait("vector", s_s, p1_ss)
        for _i4 in range(4):
            ph.wait("vector", d_x[_i4], d_x[_i4].n)
            ph.wait("vector", d_xa[_i4], p1_dxa[_i4])
        ph.op("vector", lambda e: e.memset(v_sb[:], 1.0), inc=s_v)
        v_ones_done = s_v.n

        pj_uses = []
        nuse = 0
        xat_consumed_chk = {}
        xat_w_chk = {}
        qts_chk = {}

        norm_banks = [ps_w3, ps_bcj, ps_o[0], ps_o[1]]
        norm_st = {"u": 0, "mm_chk": 0, "red_chk": bvcp, "buf": []}

        def emit_norm(idx_qm, dst_ap):
            norm_st["buf"].append((idx_qm, dst_ap))
            norm_st["u"] += 1
            if norm_st["u"] % 4 == 0:
                _fs, _fc = pj_uses[-1]
                ph.wait("vector", _fs, _fc)
                if norm_st["mm_chk"]:
                    ph.wait("vector", s_t, norm_st["mm_chk"])
                for j2, (idx, dst) in enumerate(norm_st["buf"]):
                    ph.op("vector", lambda e, j2=j2, dst=dst:
                          e.tensor_mul(sq4_sb[:, j2 * 512:(j2 + 1) * 512], dst, dst),
                          inc=(s_v if j2 == 3 else None))
                sq_done = s_v.n
                ph.wait("tensor", s_v, sq_done)
                if norm_st["red_chk"]:
                    ph.wait("tensor", s_v, norm_st["red_chk"])
                for j2 in range(4):
                    ph.op("tensor", lambda e, j2=j2:
                          e.matmul(norm_banks[j2][0:1, 0:512], ones128b[:],
                                   sq4_sb[:, j2 * 512:(j2 + 1) * 512],
                                   start=True, stop=True),
                          inc=(s_t if j2 == 3 else None))
                norm_st["mm_chk"] = s_t.n
                ph.wait("vector", s_t, norm_st["mm_chk"])
                for j2, (idx, dst) in enumerate(norm_st["buf"]):
                    ph.op("vector", lambda e, j2=j2, idx=idx:
                          e.reduce_max(qm_sb[0:1, idx:idx + 1],
                                       norm_banks[j2][0:1, 0:512], axis=AX.X),
                          inc=(s_v if j2 == 3 else None))
                norm_st["red_chk"] = s_v.n
                norm_st["buf"] = []

        for ch in range(QC):
            pxt = ch % 2
            if ch >= 2:
                ph.wait("sync", s_pjt, xat_consumed_chk[ch - 2])
            last_tile = ch * 4 + 3
            for t_ in range(max(0, last_tile - 3), last_tile + 1):
                ph.wait("sync", d_xa[t_ % 4], xa_w_chk[t_])
            for c in range(CT):
                ph.op("sync", lambda e, o=xat_sb[:, (pxt * CT + c) * 512:(pxt * CT + c + 1) * 512],
                      a=xa_hbm[ch * 512:(ch + 1) * 512, c * 128:(c + 1) * 128]:
                      e.dma_start_transpose(out=o, in_=a), inc=d_xat[pxt], k=16)
            xat_w_chk[ch] = d_xat[pxt].n
            ph.wait("tensor", d_xat[pxt], xat_w_chk[ch])
            ph.wait("tensor", d_w1, w1_loaded)
            ph.wait("tensor", s_v, fold1["done"])
            for (wsb, is_q) in ((wq_sb, True), (wk_sb, False)):
                for o in range(NP):
                    bank = nuse % 4
                    if nuse >= 4:
                        _fs, _fc = pj_uses[nuse - 4]
                        ph.wait("tensor", _fs, _fc)
                    for c in range(CT):
                        ph.op("tensor", lambda e, b=bank, w=wsb, c=c, o=o, pxt=pxt,
                              st=(c == 0), sp=(c == CT - 1):
                              e.matmul(ps_sc[:, b * 512:(b + 1) * 512],
                                       w[:, c * D + o * 128: c * D + o * 128 + 128],
                                       xat_sb[:, (pxt * CT + c) * 512:(pxt * CT + c + 1) * 512],
                                       start=st, stop=sp),
                              inc=(s_pjt if c == CT - 1 else None))
                    ph.wait("scalar", s_pjt)
                    if is_q:
                        u = len(qts_chk)
                        slot = u % 4
                        if u >= 4:
                            ph.wait("scalar", d_qts[slot], qts_chk[u - 4])
                        ph.op("scalar", lambda e, b=bank, slot=slot, o=o:
                              e.activation(qstage_sb[slot][:],
                                           ps_sc[:, b * 512:(b + 1) * 512], AF.Identity,
                                           bias=bqk_sb[:, o:o + 1]),
                              inc=s_pjv)
                        pj_uses.append((s_pjv, s_pjv.n))
                        emit_norm(o * QC + ch, qstage_sb[slot][:])
                        ph.wait("gpsimd", s_pjv, s_pjv.n)
                        ph.op("gpsimd", lambda e, slot=slot, o=o, ch=ch:
                              e.dma_start(out=qt_hbm[o * 128:(o + 1) * 128, ch * 512:(ch + 1) * 512],
                                          in_=qstage_sb[slot][:]), inc=d_qts[slot], k=16)
                        qts_chk[u] = d_qts[slot].n
                    else:
                        dst = kT_sb[:, o * S + ch * 512: o * S + (ch + 1) * 512]
                        ph.op("scalar", lambda e, b=bank, dst=dst, o=o:
                              e.activation(dst, ps_sc[:, b * 512:(b + 1) * 512], AF.Identity,
                                           bias=bqk_sb[:, NP + o:NP + o + 1]),
                              inc=s_pjv)
                        pj_uses.append((s_pjv, s_pjv.n))
                        emit_norm(NP * QC + o * QC + ch, dst)
                    nuse += 1
            ph.wait("tensor", s_v, fold2_done)
            if ch == 0:
                ph.wait("tensor", s_v, v_ones_done)
            for it in range(4):
                for ec in range(EC):
                    bank = nuse % 4
                    if nuse >= 4:
                        _fs, _fc = pj_uses[nuse - 4]
                        ph.wait("tensor", _fs, _fc)
                    for c in range(CT):
                        ph.op("tensor", lambda e, b=bank, c=c, it=it, ec=ec, pxt=pxt,
                              st=(c == 0), sp=(c == CT - 1):
                              e.matmul(ps_sc[:, b * 512:(b + 1) * 512],
                                       xat_sb[:, (pxt * CT + c) * 512 + it * 128:
                                              (pxt * CT + c) * 512 + it * 128 + 128],
                                       wv_sb[:, c * D + ec * 512: c * D + ec * 512 + 512],
                                       start=st, stop=sp),
                              inc=(s_pjt if c == CT - 1 else None))
                    ph.wait("vector", s_pjt)
                    itile = ch * 4 + it
                    hh0 = ec * (512 // dh)
                    ph.op("vector", lambda e, b=bank, itile=itile, hh0=hh0, ec=ec:
                          e.tensor_tensor(
                              v_sb[:, itile * HW + hh0 * 65: itile * HW + (hh0 + 8) * 65]
                              .rearrange("p (h ee) -> p h ee", ee=65)[:, :, 0:64],
                              ps_sc[:, b * 512:(b + 1) * 512]
                              .rearrange("p (h ee) -> p h ee", ee=64),
                              bvb_sb[:, ec * 512:(ec + 1) * 512]
                              .rearrange("p (h ee) -> p h ee", ee=64),
                              ALU.add),
                          inc=s_pjw)
                    pj_uses.append((s_pjw, s_pjw.n))
                    nuse += 1
            xat_consumed_chk[ch] = s_pjt.n
        pass1_pjt = s_pjt.n
        qkv_evac_done = s_pjv.n
        qkv_evw_done = s_pjw.n
        qkv_mm_done = s_pjt.n

        ph.drain("vector")
        for j2 in range(2 * NP):
            ph.op("vector", lambda e, j2=j2:
                  e.reduce_max(qkm_sb[0:1, j2:j2 + 1], qm_sb[0:1, j2 * QC:(j2 + 1) * QC],
                               axis=AX.X),
                  inc=(s_v if j2 == 2 * NP - 1 else None))
        ph.drain("vector")
        ph.op("vector", lambda e: e.tensor_tensor(
            prod_sb[0:1, :], qkm_sb[0:1, 0:NP], qkm_sb[0:1, NP:2 * NP], ALU.mult), inc=s_v)
        ph.drain("vector")
        ph.wait("scalar", s_v)
        ph.op("scalar", lambda e: e.activation(nmag_sb[0:1, :], prod_sb[0:1, :], AF.Ln))
        ph.drain("scalar")
        ph.op("scalar", lambda e: e.activation(nmag_sb[0:1, :], nmag_sb[0:1, :], AF.Exp,
                                               scale=0.5, bias=cln8_sb[:]), inc=s_s)
        ph.wait("vector", s_s)
        ph.op("vector", lambda e: e.tensor_scalar_mul(nmag_sb[0:1, :], nmag_sb[0:1, :], -1.0),
              inc=s_v)
        nmag_chk = s_v.n
        bias_copy_chk = 0
        for j in range(NP):
            ph.wait("tensor", s_v, nmag_chk)
            if j > 0:
                ph.wait("tensor", s_v, bias_copy_chk)
            ph.op("tensor", lambda e, j=j:
                  e.matmul(ps_o[0][0:128, 0:1], ones2f[0:1, :], nmag_sb[0:1, j:j + 1],
                           start=True, stop=True), inc=s_t)
            ph.wait("vector", s_t)
            ph.op("vector", lambda e, j=j:
                  e.tensor_copy(bias_cols[:, j:j + 1], ps_o[0][0:128, 0:1]), inc=s_v)
            bias_copy_chk = s_v.n
        bias_ready = s_v.n

        r_w1.close()
        r_xat.close()

        exp_sb = Pc(nc.sbuf_tensor("exp_sb", [128, 3072], BF16))
        qtc_sb = Pc(nc.sbuf_tensor("qtc_sb", [128, NP * 512], BF16))
        stage_sb = [Pc(nc.sbuf_tensor(f"stage{i}", [128, 512], F32)) for i in range(4)]
        attnb_sb = [Pc(nc.sbuf_tensor(f"attnb{i}", [64, 512], BF16)) for i in range(2)]

        p3sh = ExitStack()
        Ps = p3sh.enter_context
        st3_sb = [Ps(nc.sbuf_tensor(f"st3_{i}", [128, 12], F32, side="right")) for i in range(2)]
        mv3_sb = [Ps(nc.sbuf_tensor(f"mv3_{i}", [128, 4], F32, side="right")) for i in range(2)]
        rs3_sb = [Ps(nc.sbuf_tensor(f"rs3_{i}", [128, 4], F32, side="right")) for i in range(2)]
        p3a = ExitStack()
        Pa = p3a.enter_context
        wo_sb = Pa(nc.sbuf_tensor("wo_sb", [128, CT * D], BF16, side="right"))
        at3_sb = Pa(nc.sbuf_tensor("at3_sb", [128, CT * 512], BF16, side="right"))
        x3_sb = Pa(nc.sbuf_tensor("x3_sb", [128, 2 * D], F32, side="right"))
        xn_sb = Pa(nc.sbuf_tensor("xn_sb", [128, 2 * D], BF16, side="right"))
        x1r_sb = Pa(nc.sbuf_tensor("x1r_sb", [128, 3 * D], F32, side="right"))
        wo3_sb = Pa(nc.sbuf_tensor("wo3_sb", [128, D], F32, side="right"))

        ph.wait("gpsimd", s_pjt, qkv_mm_done)
        ph.wait("gpsimd", s_pjv, qkv_evac_done)
        ph.wait("gpsimd", s_pjw, qkv_evw_done)
        ph.wait("gpsimd", s_t, s_t.n)
        ph.wait("gpsimd", s_v, s_v.n)
        ph.wait("gpsimd", d_xat[0], d_xat[0].n)
        ph.wait("gpsimd", d_xat[1], d_xat[1].n)
        for c in range(CT):
            ph.op("gpsimd", lambda e, o=wo_sb[:, c * D:(c + 1) * D],
                  a=w_hbms["wo"][c * 128:(c + 1) * 128, :]:
                  e.dma_start(out=o, in_=a), inc=d_wo, k=16)
        wo_loaded = d_wo.n
        ph.wait("vector", s_pjv, qkv_evac_done)
        ph.wait("vector", s_pjw, qkv_evw_done)
        ph.wait("vector", s_pjt, qkv_mm_done)
        ph.wait("vector", d_wv, wv_loaded)
        ph.wait("vector", d_xat[0], d_xat[0].n)
        ph.wait("vector", d_xat[1], d_xat[1].n)
        ph.op("vector", lambda e: e.memset(xn_sb[0:16, 0:D], 0.0), inc=s_v)
        z16_chk = s_v.n
        ph.wait("gpsimd", s_v, z16_chk)
        ph.op("gpsimd", lambda e: e.dma_start(out=xn_hbm[0:16, :], in_=xn_sb[0:16, 0:D]),
              inc=d_xnw[0], k=16)
        z16_dma = d_xnw[0].n

        attn_iter = 0
        pending = [None]
        tail_mm2 = [None]
        prev_chk = None
        attn_free_chk = {}
        ps_o_free_chk = {}
        attn_w_chk = {}
        attn_spill_after_qc = {}
        qtr_chk = {}
        sct_before_qc = {}

        wo_t = {}
        woc_chk = {}
        a3c = {}
        r3c = {}
        x1c = {}
        a4c = {}
        r4c = {}
        xnc = {}
        x1w_chk = {}
        xnw_chk = {}
        at3_chk = {}
        x3_chk = {}

        def p3a_loads(ch):
            d0, d1 = attn_spill_after_qc[ch]
            ph.wait("sync", d_attn[0], d0)
            ph.wait("sync", d_attn[1], d1)
            if ch >= 1:
                ph.wait("sync", s_t, wo_t[(ch - 1) * 4 + 3])
            for c in range(CT):
                ph.op("sync", lambda e, o=at3_sb[:, c * 512:(c + 1) * 512],
                      a=attn_hbm[c * 128:(c + 1) * 128, ch * 512:(ch + 1) * 512]:
                      e.dma_start(out=o, in_=a), inc=d_at3[ch % 2], k=16)
            at3_chk[ch] = d_at3[ch % 2].n

        def p3a_x3load(ti):
            if ti >= 2:
                ph.wait("sync", s_v, x1c[ti - 2])
            ph.op("sync", lambda e, ti=ti, sl=(ti % 2) * D:
                  e.dma_start(out=x3_sb[:, sl:sl + D],
                              in_=x_h[ti * 128:(ti + 1) * 128, :]),
                  inc=d_x3[ti % 2], k=16)
            x3_chk[ti] = d_x3[ti % 2].n

        def p3a_T_ec(ti, ec):
            ch, it = ti // 4, ti % 4
            if ec == 0:
                ph.wait("tensor", d_at3[ch % 2], at3_chk[ch])
                ph.wait("tensor", d_wo, wo_loaded)
            if ti > 0 or ec > 0:
                ph.wait("tensor", s_v,
                        woc_chk[(ti, ec - 1) if ec > 0 else (ti - 1, EC - 1)])
            for c in range(CT):
                ph.op("tensor", lambda e, c=c, it=it, ec=ec,
                      st=(c == 0), sp=(c == CT - 1):
                      e.matmul(ps_w3[:],
                               at3_sb[:, c * 512 + it * 128: c * 512 + it * 128 + 128],
                               wo_sb[:, c * D + ec * 512: c * D + ec * 512 + 512],
                               start=st, stop=sp),
                      inc=(s_t if c == CT - 1 else None))
            wo_t[ti] = s_t.n
            ph.wait("vector", s_t, wo_t[ti])
            ph.op("vector", lambda e, ec=ec:
                  e.tensor_copy(wo3_sb[:, ec * 512:(ec + 1) * 512], ps_w3[:]), inc=s_v)
            woc_chk[(ti, ec)] = s_v.n

        def p3a_T(ti):
            for ec in range(EC):
                p3a_T_ec(ti, ec)

        def p3a_Av(ti):
            ppw = ti % 2
            ph.drain("vector")
            for half in range(EC):
                ph.op("vector", lambda e, half=half, ppw=ppw:
                      e.bn_stats(st3_sb[ppw][:, half * 6:half * 6 + 6],
                                 wo3_sb[:, half * 512:(half + 1) * 512]))
                ph.drain("vector")
            ph.op("vector", lambda e, ppw=ppw:
                  e.bn_aggr(mv3_sb[ppw][:, 0:2], st3_sb[ppw][:, 0:6 * EC]), inc=s_v)
            a3c[ti] = s_v.n

        def p3a_As(ti):
            ppw = ti % 2
            ph.wait("scalar", s_v, a3c[ti])
            ph.op("scalar", lambda e, ppw=ppw:
                  e.activation(rs3_sb[ppw][:, 0:1], mv3_sb[ppw][:, 1:2], AF.Ln, bias=ceps_sb[:]))
            ph.drain("scalar")
            ph.op("scalar", lambda e, ppw=ppw:
                  e.activation(rs3_sb[ppw][:, 0:1], rs3_sb[ppw][:, 0:1], AF.Exp, scale=-0.5),
                  inc=s_s)
            r3c[ti] = s_s.n

        def p3a_Bv(ti):
            ppw = ti % 2
            off = (ti % 3) * D
            x3o = (ti % 2) * D
            ph.wait("vector", s_s, r3c[ti])
            ph.wait("vector", d_x3[ti % 2], x3_chk[ti])
            if ti >= 3:
                ph.wait("vector", d_x1w[(ti - 3) % 4], x1w_chk[ti - 3])
            ph.op("vector", lambda e, ppw=ppw, off=off:
                  e.tensor_scalar(x1r_sb[:, off:off + D], wo3_sb[:, 0:D],
                                  mv3_sb[ppw][:, 0:1], rs3_sb[ppw][:, 0:1], ALU.subtract, ALU.mult))
            ph.drain("vector")
            ph.op("vector", lambda e, off=off, x3o=x3o:
                  e.tensor_tensor(x1r_sb[:, off:off + D], x1r_sb[:, off:off + D],
                                  x3_sb[:, x3o:x3o + D], ALU.add), inc=s_v)
            x1c[ti] = s_v.n
            ph.drain("vector")
            for half in range(EC):
                ph.op("vector", lambda e, half=half, off=off, ppw=ppw:
                      e.bn_stats(st3_sb[ppw][:, half * 6:half * 6 + 6],
                                 x1r_sb[:, off + half * 512:off + (half + 1) * 512]))
                ph.drain("vector")
            ph.op("vector", lambda e, ppw=ppw:
                  e.bn_aggr(mv3_sb[ppw][:, 2:4], st3_sb[ppw][:, 0:6 * EC]), inc=s_v)
            a4c[ti] = s_v.n

        def p3a_Bs(ti):
            ppw = ti % 2
            ph.wait("scalar", s_v, a4c[ti])
            ph.op("scalar", lambda e, ppw=ppw:
                  e.activation(rs3_sb[ppw][:, 1:2], mv3_sb[ppw][:, 3:4], AF.Ln, bias=ceps_sb[:]))
            ph.drain("scalar")
            ph.op("scalar", lambda e, ppw=ppw:
                  e.activation(rs3_sb[ppw][:, 1:2], rs3_sb[ppw][:, 1:2], AF.Exp, scale=-0.5),
                  inc=s_s)
            r4c[ti] = s_s.n

        def p3a_C(ti):
            ppw = ti % 2
            off = (ti % 3) * D
            xno = (ti % 2) * D
            ph.wait("vector", s_s, r4c[ti])
            if ti == 0:
                ph.wait("vector", d_xnw[0], z16_dma)
            if ti >= 2:
                ph.wait("vector", d_xnw[ti % 2], xnw_chk[ti - 2])
            ph.op("vector", lambda e, off=off, xno=xno, ppw=ppw:
                  e.tensor_scalar(xn_sb[:, xno:xno + D], x1r_sb[:, off:off + D],
                                  mv3_sb[ppw][:, 2:3], rs3_sb[ppw][:, 1:2], ALU.subtract, ALU.mult),
                  inc=s_v)
            xnc[ti] = s_v.n
            ph.wait("gpsimd", s_v, x1c[ti])
            ph.op("gpsimd", lambda e, ti=ti, off=off:
                  e.dma_start(out=x1_hbm[ti * 128:(ti + 1) * 128, :],
                              in_=x1r_sb[:, off:off + D]),
                  inc=d_x1w[ti % 4], k=16)
            x1w_chk[ti] = d_x1w[ti % 4].n
            ph.wait("gpsimd", s_v, xnc[ti])
            ph.op("gpsimd", lambda e, ti=ti, xno=xno:
                  e.dma_start(out=xn_hbm[16 + ti * 128:16 + (ti + 1) * 128, :],
                              in_=xn_sb[:, xno:xno + D]),
                  inc=d_xnw[ti % 2], k=16)
            xnw_chk[ti] = d_xnw[ti % 2].n

        def p3a_stepT(ti):
            p3a_x3load(ti)
            p3a_T(ti)
            p3a_Av(ti)

        NPH = NP // 2
        qtr_lo_chk = {}
        qtr_hi_chk = {}

        def qtc_load(qc, half):
            o0 = half * NPH
            ph.op("sync", lambda e, qc=qc, o0=o0:
                  e.dma_start(out=qtc_sb[:, o0 * 512:(o0 + NPH) * 512]
                              .rearrange("p (o c) -> p o c", c=512),
                              in_=qt_hbm[o0 * 128:(o0 + NPH) * 128, qc * 512:(qc + 1) * 512]
                              .rearrange("(o p) c -> p o c", p=128)),
                  inc=d_qtr[half], k=16)
            (qtr_lo_chk if half == 0 else qtr_hi_chk)[qc] = d_qtr[half].n

        for qc in range(QC):
            if qc == 0:
                ph.wait("sync", s_pjt, qkv_mm_done)
                ph.wait("sync", s_pjv, qkv_evac_done)
                ph.wait("sync", s_pjw, qkv_evw_done)
                for _s in range(4):
                    ph.wait("sync", d_qts[_s], d_qts[_s].n)
                qtc_load(0, 0)
                qtc_load(0, 1)
            else:
                ph.wait("sync", s_sct, sct_before_qc[qc])
                qtc_load(qc, 1)
            for pj in range(NP):
                idx = attn_iter
                pi = idx % 2
                if pj == NPH:
                    sct_half_mark = s_sct.n
                chk_kt = {}
                for kt in range(KT):
                    bp = (kt % 2) * 2
                    if kt >= 2:
                        ph.wait("tensor", s_exp, chk_kt[kt - 2])
                    elif prev_chk is not None:
                        ph.wait("tensor", s_exp, prev_chk[KT - 2 + kt])
                    if kt == 0:
                        ph.wait("tensor", s_v, bias_ready)
                        if qc == 0 and pj == 0:
                            ph.wait("tensor", s_pjv, qkv_evac_done)
                            ph.wait("tensor", s_pjw, qkv_evw_done)
                        if pj == 0:
                            ph.wait("tensor", d_qtr[0], qtr_lo_chk[qc])
                        elif pj == NPH:
                            ph.wait("tensor", d_qtr[1], qtr_hi_chk[qc])
                    for r2 in range(2):
                        lo = r2 * 64
                        ph.op("tensor", lambda e, bp=bp, pj=pj, lo=lo, kt=kt, r2=r2:
                              e.matmul(ps_sc[:, (bp + r2) * 512:(bp + r2 + 1) * 512],
                                       kT_sb[lo:lo + 64, pj * S + kt * 128: pj * S + (kt + 1) * 128],
                                       qtc_sb[lo:lo + 64, pj * 512:(pj + 1) * 512],
                                       start=True, stop=True),
                              inc=(s_sct if r2 == 1 else None))
                    sct_chk = s_sct.n
                    if kt == 0 and tail_mm2[0] is not None:
                        tail_mm2[0]()
                        tail_mm2[0] = None
                    if kt == min(8, KT - 1):
                        if pending[0] is not None:
                            pending[0]()
                            pending[0] = None
                        if pj == NPH and qc + 1 < QC:
                            ph.wait("sync", s_sct, sct_half_mark)
                            qtc_load(qc + 1, 0)
                        if qc >= 1:
                            t0 = 4 * (qc - 1)
                            e3 = t0 - 1
                            if NP >= 8:
                                if pj == 0:
                                    if qc >= 2:
                                        for _ec in range(1, EC):
                                            p3a_T_ec(e3, _ec)
                                        p3a_Av(e3)
                                    p3a_loads(qc - 1)
                                elif pj == 1:
                                    if qc >= 2:
                                        p3a_As(e3)
                                        p3a_Bv(e3)
                                    p3a_stepT(t0)
                                elif pj == 2:
                                    if qc >= 2:
                                        p3a_Bs(e3)
                                        p3a_C(e3)
                                elif pj in (3, 5, 7):
                                    k = (pj - 3) // 2
                                    p3a_As(t0 + k)
                                    p3a_Bv(t0 + k)
                                elif pj in (4, 6):
                                    k = pj // 2 - 2
                                    p3a_Bs(t0 + k)
                                    p3a_C(t0 + k)
                                    p3a_stepT(t0 + k + 1)
                            else:
                                if pj == 0:
                                    if qc >= 2:
                                        for _ec in range(1, EC):
                                            p3a_T_ec(e3, _ec)
                                        p3a_Av(e3)
                                        p3a_As(e3)
                                        p3a_Bv(e3)
                                    p3a_loads(qc - 1)
                                    if qc >= 2:
                                        p3a_Bs(e3)
                                        p3a_C(e3)
                                    p3a_stepT(t0)
                                else:
                                    p3a_As(t0 + pj - 1)
                                    p3a_Bv(t0 + pj - 1)
                                    p3a_Bs(t0 + pj - 1)
                                    p3a_C(t0 + pj - 1)
                                    if pj < NP - 1:
                                        p3a_stepT(t0 + pj)
                    if kt >= 2:
                        ktg = kt - 2
                        if kt == 2 and idx >= 1:
                            ph.wait("tensor", s_aev, ps_o_free_chk[idx - 1])
                        eo = (ktg % 3) * 1024
                        for r2 in range(2):
                            hh = 2 * pj + r2
                            ph.op("tensor", lambda e, ktg=ktg, hh=hh, r2=r2, eo=eo,
                                  st=(kt == 2):
                                  e.matmul(ps_o[r2][0:65, :],
                                           v_sb[:, ktg * HW + hh * 65: ktg * HW + hh * 65 + 65],
                                           exp_sb[:, eo + r2 * 512: eo + r2 * 512 + 512],
                                           start=st, stop=False))
                    ph.wait("scalar", s_sct, sct_chk)
                    if kt == 0:
                        ph.wait("scalar", s_v, bias_ready)
                        if idx >= 1:
                            ph.wait("scalar", s_mm2, 2 * idx - 1)
                    if kt == 1 and idx >= 1:
                        ph.wait("scalar", s_mm2, 2 * idx)
                    ph.op("scalar", lambda e, bp=bp, pj=pj, eo=(kt % 3) * 1024:
                          e.activation(exp_sb[:, eo:eo + 1024],
                                       ps_sc[:, bp * 512:(bp + 2) * 512],
                                       AF.Exp, bias=bias_cols[:, pj:pj + 1], scale=0.125),
                          inc=s_exp)
                    chk_kt[kt] = s_exp.n

                def mk_tail(chk30=chk_kt[KT - 2], chk31=chk_kt[KT - 1], pj=pj):
                    def emit():
                        for ktg in (KT - 2, KT - 1):
                            ph.wait("tensor", s_exp, chk30 if ktg == KT - 2 else chk31)
                            eo = (ktg % 3) * 1024
                            for r2 in range(2):
                                hh = 2 * pj + r2
                                ph.op("tensor", lambda e, ktg=ktg, hh=hh, r2=r2, eo=eo,
                                      sp=(ktg == KT - 1):
                                      e.matmul(ps_o[r2][0:65, :],
                                               v_sb[:, ktg * HW + hh * 65: ktg * HW + hh * 65 + 65],
                                               exp_sb[:, eo + r2 * 512: eo + r2 * 512 + 512],
                                               start=False, stop=sp),
                                      inc=(s_mm2 if r2 == 1 else None))
                    return emit
                tail_mm2[0] = mk_tail()
                prev_chk = chk_kt
                mm2_chk = s_mm2.n + 2
                ph.wait("vector", s_mm2, mm2_chk)
                for r2 in range(2):
                    ph.op("vector", lambda e, si=pi * 2 + r2, r2=r2:
                          e.tensor_copy(stage_sb[si][0:65, :], ps_o[r2][0:65, :]), inc=s_aev)
                ps_o_free_chk[idx] = s_aev.n
                ph.drain("vector")
                if pj == NP - 1:
                    ph.wait("scalar", s_aev, ps_o_free_chk[idx])
                    for r2 in range(2):
                        ph.op("scalar", lambda e, si=pi * 2 + r2:
                              e.activation(stage_sb[si][64:65, :], stage_sb[si][64:65, :],
                                           AF.Ln))
                    ph.drain("scalar")
                    for r2 in range(2):
                        ph.op("scalar", lambda e, si=pi * 2 + r2:
                              e.activation(stage_sb[si][64:65, :], stage_sb[si][64:65, :],
                                           AF.Exp, scale=-1.0),
                              inc=s_rec)
                else:
                    for r2 in range(2):
                        ph.op("vector", lambda e, si=pi * 2 + r2:
                              e.reciprocal(stage_sb[si][64:65, :], stage_sb[si][64:65, :]),
                              inc=s_rec)
                rec_chk = s_rec.n

                def mk_pending(rec_chk=rec_chk, pi=pi, idx=idx):
                    def emit():
                        ph.wait("tensor", s_rec, rec_chk)
                        if idx >= 1:
                            ph.wait("tensor", s_amu, 2 * idx)
                        for r2 in range(2):
                            ph.op("tensor", lambda e, si=pi * 2 + r2, r2=r2:
                                  e.matmul(ps_bc[r2], ones128f[64:65, :],
                                           stage_sb[si][64:65, :],
                                           start=True, stop=True,
                                           tile_position=(64, 64 * r2)), inc=s_bct)
                    return emit
                pending[0] = mk_pending()
                ph.wait("vector", s_bct, 2 * (idx + 1))
                if idx >= 1:
                    ph.wait("vector", d_attn[(idx - 1) % 2], attn_w_chk[idx - 1])
                for r2 in range(2):
                    ph.op("vector", lambda e, si=pi * 2 + r2, r2=r2:
                          e.tensor_tensor(attnb_sb[r2][:], stage_sb[si][0:64, :], ps_bc[r2],
                                          ALU.mult), inc=s_amu)
                attn_free_chk[idx] = s_amu.n
                ph.wait("sync", s_amu, attn_free_chk[idx])
                for r2 in range(2):
                    hg_ = 2 * pj + r2
                    ph.op("sync", lambda e, r2=r2, hg_=hg_, qc=qc:
                          e.dma_start(out=attn_hbm[hg_ * 64:(hg_ + 1) * 64, qc * 512:(qc + 1) * 512],
                                      in_=attnb_sb[r2][:]), inc=d_attn[pi], k=16)
                attn_w_chk[idx] = d_attn[pi].n
                attn_iter += 1
            if qc >= 1:
                t3b = 4 * (qc - 1) + 3
                if NP >= 8:
                    p3a_Bs(t3b - 1)
                    p3a_C(t3b - 1)
                p3a_x3load(t3b)
                p3a_T_ec(t3b, 0)
            if tail_mm2[0] is not None:
                tail_mm2[0]()
                tail_mm2[0] = None
            attn_spill_after_qc[qc] = (d_attn[0].n, d_attn[1].n)
            sct_before_qc[qc + 1] = s_sct.n
        if tail_mm2[0] is not None:
            tail_mm2[0]()
            tail_mm2[0] = None
        if pending[0] is not None:
            pending[0]()
            pending[0] = None
        NTI = 4 * QC
        if QC >= 2:
            t3p = 4 * (QC - 2) + 3
            for _ec in range(1, EC):
                p3a_T_ec(t3p, _ec)
            p3a_Av(t3p)
        p3a_loads(QC - 1)
        if QC >= 2:
            p3a_As(t3p)
            p3a_Bv(t3p)
            p3a_Bs(t3p)
            p3a_C(t3p)
        for ti in range(4 * (QC - 1), NTI):
            p3a_stepT(ti)
            p3a_As(ti)
            p3a_Bv(ti)
            p3a_Bs(ti)
            p3a_C(ti)
        p3a_xnw = (d_xnw[0].n, d_xnw[1].n)
        p2_attn_dma = (d_attn[0].n, d_attn[1].n)
        p2_exp_done = s_exp.n

        p3a.close()
        p2core.close()
        p3c = ExitStack()
        P = p3c.enter_context
        wrt_sb = P(nc.sbuf_tensor("wrt_sb", [128, CT * D], BF16))
        wkt_sb = P(nc.sbuf_tensor("wkt_sb", [128, CT * D], BF16))
        wvt_sb = P(nc.sbuf_tensor("wvt_sb", [128, CT * D], BF16))
        ph.wait("gpsimd", s_exp, p2_exp_done)
        ph.wait("gpsimd", s_amu, s_amu.n)
        ph.wait("gpsimd", s_mm2, s_mm2.n)
        ph.wait("gpsimd", s_v, s_v.n)
        ph.wait("gpsimd", s_t, s_t.n)
        ph.wait("gpsimd", s_s, s_s.n)
        ph.wait("gpsimd", d_attn[0], p2_attn_dma[0])
        ph.wait("gpsimd", d_attn[1], p2_attn_dma[1])
        ph.wait("gpsimd", d_qtr[0], d_qtr[0].n)
        ph.wait("gpsimd", d_qtr[1], d_qtr[1].n)
        ph.wait("gpsimd", d_at3[0], d_at3[0].n)
        ph.wait("gpsimd", d_at3[1], d_at3[1].n)
        ph.wait("gpsimd", d_x3[0], d_x3[0].n)
        ph.wait("gpsimd", d_x3[1], d_x3[1].n)
        for _s in range(4):
            ph.wait("gpsimd", d_x1w[_s], d_x1w[_s].n)
        ph.wait("gpsimd", d_xnw[0], p3a_xnw[0])
        ph.wait("gpsimd", d_xnw[1], p3a_xnw[1])
        for wn, w_sb in (("wrt", wrt_sb), ("wkt", wkt_sb), ("wvt", wvt_sb)):
            for c in range(CT):
                ph.op("gpsimd", lambda e, o=w_sb[:, c * D:(c + 1) * D],
                      a=w_hbms[wn][c * 128:(c + 1) * 128, :]:
                      e.dma_start(out=o, in_=a), inc=d_wf, k=16)
        w_p3c_loaded = d_wf.n
        xnT_sb = P(nc.sbuf_tensor("xnT_sb", [128, 2 * CT * 512], BF16))
        xrT_sb = P(nc.sbuf_tensor("xrT_sb", [128, 2 * CT * 512], BF16))
        xkT_sb = P(nc.sbuf_tensor("xkT_sb", [128, 2 * CT * 512], BF16))
        dmix_sb = P(nc.sbuf_tensor("dmix_sb", [128, 2 * 512], BF16))
        kact_sb = P(nc.sbuf_tensor("kact_sb", [128, CT * 512], BF16))
        k1_sb = P(nc.sbuf_tensor("k1_sb", [128, 2 * 512], BF16))
        rt_sb = P(nc.sbuf_tensor("rt_sb", [128, 2 * 4 * D], BF16))
        g1_sb = P(nc.sbuf_tensor("g1_sb", [128, D], BF16))
        gg_sb = P(nc.sbuf_tensor("gg_sb", [128, 2 * D], F32))
        ob_sb = P(nc.sbuf_tensor("ob_sb", [128, 2 * 4 * D], F32))
        x1c_sb = P(nc.sbuf_tensor("x1c_sb", [128, 4 * D], F32))
        ps_k3 = [P(nc.psum_tensor(f"ps_k3{i}", [128, 512], F32)) for i in range(2)]
        ps_wo = [P(nc.psum_tensor(f"ps_wo{i}", [128, 1024], F32)) for i in range(3)]
        pwo_use = [0]
        pwo_free = {}
        pwo_bank = {}

        sync_p3c_barrier = [False]

        mix_done = {}
        rk_done = {}
        out_done = {}
        rtanh_chk = {}
        krelu_chk = {}
        ksq_chk = {}
        v3_chk = {}
        tr_chk = {}
        x1r_chk2 = {}
        out_w_chk = {}

        def p3c_trloads(ch):
            pp = ch % 2
            if not sync_p3c_barrier[0]:
                sync_p3c_barrier[0] = True
                ph.wait("sync", s_amu, s_amu.n)
                ph.wait("sync", s_exp, p2_exp_done)
                ph.wait("sync", d_xnw[0], p3a_xnw[0])
                ph.wait("sync", d_xnw[1], p3a_xnw[1])
            if ch >= 2:
                ph.wait("sync", s_v, mix_done[ch - 1])
            for c in range(CT):
                ph.op("sync", lambda e, o=xnT_sb[:, (pp * CT + c) * 512:(pp * CT + c + 1) * 512],
                      a=xn_hbm[16 + ch * 512:16 + (ch + 1) * 512, c * 128:(c + 1) * 128]:
                      e.dma_start_transpose(out=o, in_=a), inc=d_tr3[pp], k=16)
            tr_chk[ch] = d_tr3[pp].n

        def p3c_x1load(ch):
            pp = ch % 2
            if ch >= 1:
                ph.wait("gpsimd", s_v, out_done[ch - 1])
            for t in range(ch * 4, ch * 4 + 4):
                ph.wait("gpsimd", d_x1w[t % 4], x1w_chk[t])
            ph.op("gpsimd", lambda e, ch=ch:
                  e.dma_start(out=x1c_sb[:].rearrange("p (i d) -> p i d", d=D),
                              in_=x1_hbm[ch * 512:(ch + 1) * 512, :].rearrange("(i p) d -> p i d", p=128)),
                  inc=d_x1r[pp], k=16)
            x1r_chk2[ch] = d_x1r[pp].n

        def p3c_mixes(ch):
            pp = ch % 2
            ph.wait("vector", d_tr3[pp], tr_chk[ch])
            ph.wait("vector", d_wc, consts_loaded)
            if ch >= 2:
                ph.wait("vector", s_t, rk_done[ch - 2])
            for c in range(CT):
                dm = dmix_sb[:, (c % 2) * 512:(c % 2) * 512 + 512]
                xn_sl = xnT_sb[:, (pp * CT + c) * 512:(pp * CT + c + 1) * 512]
                seam = (zcol_sb[:] if ch == 0 else
                        xnT_sb[:, ((1 - pp) * CT + c) * 512 + 511:((1 - pp) * CT + c) * 512 + 512])
                ph.op("vector", lambda e, a=xn_sl, dm=dm: e.tensor_sub(dm[:, 1:512], a[:, 1:512], a[:, 0:511]))
                ph.op("vector", lambda e, a=xn_sl, sm=seam, dm=dm: e.tensor_sub(dm[:, 0:1], a[:, 0:1], sm))
                ph.drain("vector")
                xro = (pp * CT + c) * 512
                ph.op("vector", lambda e, c=c, a=xn_sl, xro=xro, dm=dm:
                      e.scalar_tensor_tensor(xrT_sb[:, xro + 1:xro + 512], dm[:, 1:512],
                                             mur_sb[:, c:c + 1], a[:, 0:511], ALU.mult, ALU.add))
                ph.op("vector", lambda e, c=c, sm=seam, xro=xro, dm=dm:
                      e.scalar_tensor_tensor(xrT_sb[:, xro:xro + 1], dm[:, 0:1],
                                             mur_sb[:, c:c + 1], sm, ALU.mult, ALU.add))
                ph.op("vector", lambda e, c=c, a=xn_sl, xro=xro, dm=dm:
                      e.scalar_tensor_tensor(xkT_sb[:, xro + 1:xro + 512], dm[:, 1:512],
                                             muk_sb[:, c:c + 1], a[:, 0:511], ALU.mult, ALU.add))
                ph.op("vector", lambda e, c=c, sm=seam, xro=xro, dm=dm:
                      e.scalar_tensor_tensor(xkT_sb[:, xro:xro + 1], dm[:, 0:1],
                                             muk_sb[:, c:c + 1], sm, ALU.mult, ALU.add),
                      inc=(s_v if c == CT - 1 else None))
            mix_done[ch] = s_v.n

        def p3c_rproj(ch):
            pp = ch % 2
            for it in range(4):
                m = pwo_use[0]
                ppw = m % 3
                ph.wait("tensor", s_v, mix_done[ch])
                ph.wait("tensor", d_wf, w_p3c_loaded)
                if m >= 3:
                    fsem, fcnt = pwo_free[m - 3]
                    ph.wait("tensor", fsem, fcnt)
                for ec in range(EC):
                    for c in range(CT):
                        ph.op("tensor", lambda e, ppw=ppw, c=c, it=it, ec=ec, pp=pp,
                              st=(c == 0), sp=(c == CT - 1):
                              e.matmul(ps_wo[ppw][:, ec * 512:(ec + 1) * 512],
                                       xrT_sb[:, (pp * CT + c) * 512 + it * 128:
                                              (pp * CT + c) * 512 + it * 128 + 128],
                                       wrt_sb[:, c * D + ec * 512: c * D + ec * 512 + 512],
                                       start=st, stop=sp),
                              inc=(s_t if (c == CT - 1 and ec == EC - 1) else None))
                r_chk = s_t.n
                ph.wait("scalar", s_t, r_chk)
                if ch >= 2:
                    ph.wait("scalar", s_gm, (ch - 2) * 4 + it + 1)
                ph.op("scalar", lambda e, it=it, ppw=ppw, pp=pp:
                      e.activation(rt_sb[:, (pp * 4 + it) * D:(pp * 4 + it + 1) * D],
                                   ps_wo[ppw][:, 0:D], AF.Tanh, scale=0.5), inc=s_s)
                rtanh_chk[ch * 4 + it] = s_s.n
                pwo_free[m] = (s_s, rtanh_chk[ch * 4 + it])
                pwo_use[0] += 1

        def p3c_kproj(ch):
            pp = ch % 2
            for o in range(CT):
                fo = ch * CT + o
                ppk = o % 2
                ph.wait("tensor", s_v, mix_done[ch])
                if fo >= 2:
                    ph.wait("tensor", s_s, krelu_chk[fo - 2])
                for c in range(CT):
                    ph.op("tensor", lambda e, ppk=ppk, c=c, o=o, pp=pp,
                          st=(c == 0), sp=(c == CT - 1):
                          e.matmul(ps_k3[ppk][:],
                                   wkt_sb[:, c * D + o * 128: c * D + o * 128 + 128],
                                   xkT_sb[:, (pp * CT + c) * 512:(pp * CT + c + 1) * 512],
                                   start=st, stop=sp),
                          inc=(s_t if c == CT - 1 else None))
                k_chk = s_t.n
                ph.wait("scalar", s_t, k_chk)
                if fo >= 2:
                    ph.wait("scalar", s_v, ksq_chk[fo - 2])
                ph.op("scalar", lambda e, ppk=ppk:
                      e.activation(k1_sb[:, ppk * 512:(ppk + 1) * 512], ps_k3[ppk][:], AF.Relu),
                      inc=s_s)
                krelu_chk[fo] = s_s.n
                ph.wait("vector", s_s, krelu_chk[fo])
                ph.op("vector", lambda e, o=o, ppk=ppk:
                      e.tensor_mul(kact_sb[:, o * 512:(o + 1) * 512],
                                   k1_sb[:, ppk * 512:(ppk + 1) * 512],
                                   k1_sb[:, ppk * 512:(ppk + 1) * 512]), inc=s_v)
                ksq_chk[fo] = s_v.n
            rk_done[ch] = s_t.n

        def p3c_vout(ch):
            kd = ksq_chk[ch * CT + CT - 1]
            for it in range(4):
                m = pwo_use[0]
                ppw = m % 3
                ph.wait("tensor", s_v, kd)
                if m >= 3:
                    fsem, fcnt = pwo_free[m - 3]
                    ph.wait("tensor", fsem, fcnt)
                for ec in range(EC):
                    for c in range(CT):
                        ph.op("tensor", lambda e, ppw=ppw, c=c, it=it, ec=ec,
                              st=(c == 0), sp=(c == CT - 1):
                              e.matmul(ps_wo[ppw][:, ec * 512:(ec + 1) * 512],
                                       kact_sb[:, c * 512 + it * 128: c * 512 + it * 128 + 128],
                                       wvt_sb[:, c * D + ec * 512: c * D + ec * 512 + 512],
                                       start=st, stop=sp),
                              inc=(s_t if (c == CT - 1 and ec == EC - 1) else None))
                v3_chk[ch * 4 + it] = s_t.n
                pwo_free[m] = (s_gm, ch * 4 + it + 1)
                pwo_bank[("v", ch, it)] = ppw
                pwo_use[0] += 1

        def p3c_gates(ch):
            pp = ch % 2
            ob_chks = {}

            def GA(it):
                ppw = it % 2
                ph.wait("scalar", s_s, rtanh_chk[ch * 4 + it])
                if ch * 4 + it >= 1:
                    ph.wait("scalar", s_gm, ch * 4 + it)
                ph.op("scalar", lambda e, it=it:
                      e.activation(g1_sb[:], rt_sb[:, (pp * 4 + it) * D:(pp * 4 + it + 1) * D],
                                   AF.Identity, scale=0.5, bias=ch05_sb[:]), inc=s_s)
                g1_chk = s_s.n
                ph.wait("vector", s_t, v3_chk[ch * 4 + it])
                ph.wait("vector", s_s, g1_chk)
                if it >= 2:
                    ph.wait("vector", s_s, ob_chks[it - 2])
                vb = pwo_bank[("v", ch, it)]
                gs = gg_sb[:, (it % 2) * D:(it % 2) * D + D]
                ph.op("vector", lambda e, vb=vb, gs=gs:
                      e.tensor_tensor(gs, g1_sb[:], ps_wo[vb][:, 0:D], ALU.mult),
                      inc=s_gm)
                ph.drain("vector")
                for half in range(EC):
                    ph.op("vector", lambda e, half=half, ppw=ppw, gs=gs:
                          e.bn_stats(st3_sb[ppw][:, half * 6:half * 6 + 6],
                                     gs[:, half * 512:(half + 1) * 512]))
                ph.drain("vector")
                ph.op("vector", lambda e, ppw=ppw:
                      e.bn_aggr(mv3_sb[ppw][:, 0:2], st3_sb[ppw][:, 0:6 * EC]), inc=s_v)

            def GB(it):
                ppw = it % 2
                ph.wait("scalar", s_v)
                ph.op("scalar", lambda e, ppw=ppw:
                      e.activation(rs3_sb[ppw][:, 2:3], mv3_sb[ppw][:, 1:2], AF.Ln, bias=ceps_sb[:]))
                ph.drain("scalar")
                ph.op("scalar", lambda e, ppw=ppw:
                      e.activation(rs3_sb[ppw][:, 2:3], rs3_sb[ppw][:, 2:3], AF.Exp, scale=-0.5))
                ph.drain("scalar")
                ph.op("scalar", lambda e, ppw=ppw, mc=(it % 2):
                      e.activation(ms_sb[:, mc:mc + 1], mv3_sb[ppw][:, 0:1], AF.Copy,
                                   scale=rs3_sb[ppw][:, 2:3]))
                ph.drain("scalar")
                ph.op("scalar", lambda e, mc=(it % 2):
                      e.activation(nms_sb[:, mc:mc + 1], ms_sb[:, mc:mc + 1], AF.Copy, scale=-1.0))
                ph.drain("scalar")
                off = pp * 4 * D + it * D
                if it == 0 and ch >= 2:
                    ph.wait("scalar", d_out[pp], out_w_chk[ch - 2])
                gs = gg_sb[:, (it % 2) * D:(it % 2) * D + D]
                ph.op("scalar", lambda e, off=off, ppw=ppw, gs=gs, mc=(it % 2):
                      e.activation(ob_sb[:, off:off + D], gs, AF.Identity,
                                   scale=rs3_sb[ppw][:, 2:3], bias=nms_sb[:, mc:mc + 1]),
                      inc=s_s)
                ob_chks[it] = s_s.n

            def GC(it):
                off = pp * 4 * D + it * D
                ph.wait("vector", s_s, ob_chks[it])
                ph.wait("vector", d_x1r[pp], x1r_chk2[ch])
                ph.op("vector", lambda e, off=off, it=it:
                      e.tensor_tensor(ob_sb[:, off:off + D], ob_sb[:, off:off + D],
                                      x1c_sb[:, it * D:(it + 1) * D], ALU.add), inc=s_v)

            for st_i in range(6):
                if st_i < 4:
                    GA(st_i)
                if 1 <= st_i < 5:
                    GB(st_i - 1)
                if st_i >= 2:
                    GC(st_i - 2)
            out_done[ch] = s_v.n
            ph.wait("sync", s_v, out_done[ch])
            ph.op("sync", lambda e, pp=pp, ch=ch:
                  e.dma_start(out=out_h[ch * 512:(ch + 1) * 512, :].rearrange("(i p) d -> p i d", p=128),
                              in_=ob_sb[:, pp * 4 * D:(pp + 1) * 4 * D].rearrange("p (i d) -> p i d", d=D)),
                  inc=d_out[pp], k=16)
            out_w_chk[ch] = d_out[pp].n

        for ch in range(QC):
            p3c_trloads(ch)
            p3c_mixes(ch)
            if ch >= 1:
                p3c_gates(ch - 1)
            p3c_x1load(ch)
            p3c_rproj(ch)
            p3c_kproj(ch)
            p3c_vout(ch)
        p3c_gates(QC - 1)
        ph.wait("sync", d_out[0], d_out[0].n)
        ph.wait("sync", d_out[1], d_out[1].n)

        ph.replay(block)
        p3c.close()
        p3sh.close()
    return nc


def _prep_inputs(L0_attn_latents, L0_attn_ln_gamma, L0_attn_ln_beta, L0_attn_Wq,
                 L0_attn_Wk, L0_attn_Wv, L0_attn_Wo, L0_ffn_W_r, L0_ffn_W_k,
                 L0_ffn_W_v, L0_ffn_mu_r, L0_ffn_mu_k):
    D = np.asarray(L0_attn_Wq).shape[0]
    CT = D // 128
    f = lambda a: np.ascontiguousarray(np.asarray(a, dtype=np.float32))
    com = {
        "wq": f(L0_attn_Wq), "wk": f(L0_attn_Wk), "wv": f(L0_attn_Wv), "wo": f(L0_attn_Wo),
        "wrt": f(np.asarray(L0_ffn_W_r).T), "wkt": f(np.asarray(L0_ffn_W_k).T),
        "wvt": f(np.asarray(L0_ffn_W_v).T),
        "gamma": f(np.asarray(L0_attn_ln_gamma).reshape(CT, 128).T),
        "beta": f(np.asarray(L0_attn_ln_beta).reshape(CT, 128).T),
        "mur": f(np.asarray(L0_ffn_mu_r).reshape(CT, 128).T),
        "muk": f(np.asarray(L0_ffn_mu_k).reshape(CT, 128).T),
    }
    x = f(L0_attn_latents)
    return x, com


def kernel(**inputs):
    x, com = _prep_inputs(**inputs)
    B, S, D = x.shape
    nc = build_nc(S=S, D=D, H=D // 64, dh=64)
    in_maps = [dict(com, x=np.ascontiguousarray(x[b])) for b in range(B)]
    res = run_bass_kernel_spmd(nc, in_maps, core_ids=list(range(B)))
    return np.stack([np.asarray(r["out"], dtype=np.float32) for r in res.results], axis=0)


# version 26
# speedup vs baseline: 1.0410x; 1.0410x over previous
import numpy as np
from contextlib import ExitStack

import concourse.bass as bass
import concourse.mybir as mybir
from concourse.bass_utils import run_bass_kernel_spmd

F32 = mybir.dt.float32
BF16 = mybir.dt.bfloat16
AF = mybir.ActivationFunctionType
ALU = mybir.AluOpType
AX = mybir.AxisListType

EPS = 1e-5
LN8 = float(np.log(8.0))

ENGS = ("sync", "scalar", "vector", "tensor", "gpsimd")


class Sem:
    def __init__(self, h):
        self.h = h
        self.n = 0


class Emit:

    def __init__(self):
        self.q = {k: [] for k in ENGS}

    def op(self, eng, fn, inc=None, k=1):
        if inc is not None:
            self.q[eng].append(lambda e, fn=fn, s=inc.h, k=k: fn(e).then_inc(s, k))
            inc.n += k
        else:
            self.q[eng].append(fn)

    def wait(self, eng, sem, val=None):
        v = sem.n if val is None else val
        if v > 0:
            self.q[eng].append(lambda e, s=sem.h, v=v: e.wait_ge(s, v))

    def drain(self, eng):
        self.q[eng].append(lambda e: e.drain())

    def replay(self, block):
        for eng in ENGS:
            ops = self.q[eng]
            if not ops:
                continue
            getattr(block, eng)(lambda b, ops=ops: [f(b) for f in ops])


def build_nc(S=4096, D=1024, H=16, dh=64):
    CT = D // 128
    NT = S // 128
    QC = S // 512
    KT = S // 128
    NP = H // 2
    EC = D // 512
    HW = H * 65

    nc = bass.Bass("TRN2", target_bir_lowering=False, debug=False)

    x_h = nc.dram_tensor("x", [S, D], F32, kind="ExternalInput").ap()
    w_hbms = {n: nc.dram_tensor(n, [D, D], F32, kind="ExternalInput").ap()
              for n in ("wq", "wk", "wv", "wo", "wrt", "wkt", "wvt")}
    gam_h = nc.dram_tensor("gamma", [128, D // 128], F32, kind="ExternalInput").ap()
    bet_h = nc.dram_tensor("beta", [128, D // 128], F32, kind="ExternalInput").ap()
    mur_h = nc.dram_tensor("mur", [128, CT], F32, kind="ExternalInput").ap()
    muk_h = nc.dram_tensor("muk", [128, CT], F32, kind="ExternalInput").ap()
    out_h = nc.dram_tensor("out", [S, D], F32, kind="ExternalOutput").ap()

    xa_hbm = nc.dram_tensor("xa_h", [S, D], BF16, kind="Internal").ap()
    qt_hbm = nc.dram_tensor("qt_h", [NP * 128, S], BF16, kind="Internal").ap()
    attn_hbm = nc.dram_tensor("attn_h", [D, S], BF16, kind="Internal").ap()
    x1_hbm = nc.dram_tensor("x1_h", [S, D], F32, kind="Internal").ap()
    xn_hbm = nc.dram_tensor("xn_h", [S + 16, D], BF16, kind="Internal").ap()

    with ExitStack() as top:
        E = top.enter_context

        def sem(name):
            return Sem(E(nc.semaphore(name)))

        def sem2(name):
            return [Sem(E(nc.semaphore(name + "0"))), Sem(E(nc.semaphore(name + "1")))]

        def sem4(name):
            return [Sem(E(nc.semaphore(name + str(i)))) for i in range(4)]

        d_x = sem4("d_x")
        d_wc = sem("d_wc")
        d_w1 = sem("d_w1")
        d_wv = sem("d_wv")
        d_wo = sem("d_wo")
        d_wf = sem("d_wf")
        d_xa = sem4("d_xa")
        d_xat = sem2("d_xat")
        d_qts = sem4("d_qts")
        d_qtr = sem2("d_qtr")
        d_attn = sem2("d_attn")
        d_at3 = sem2("d_at3")
        d_x3 = sem2("d_x3")
        d_x1w = sem4("d_x1w")
        d_xnw = sem2("d_xnw")
        d_tr3 = sem2("d_tr3")
        d_x1r = sem2("d_x1r")
        d_out = sem2("d_out")

        s_v = sem("s_v")
        s_s = sem("s_s")
        s_t = sem("s_t")
        s_sct = sem("s_sct")
        s_exp = sem("s_exp")
        s_mm2 = sem("s_mm2")
        s_aev = sem("s_aev")
        s_rec = sem("s_rec")
        s_bct = sem("s_bct")
        s_amu = sem("s_amu")
        s_pjt = sem("s_pjt")
        s_pjw = sem("s_pjw")
        s_pjv = sem("s_pjv")
        s_gm = sem("s_gm")

        block = E(nc.Block())

        perm = E(ExitStack())
        P = perm.enter_context
        ones2f = P(nc.sbuf_tensor("ones2f", [2, 128], F32))
        ones1b = P(nc.sbuf_tensor("ones1b", [1, 128], BF16))
        ones128b = P(nc.sbuf_tensor("ones128b", [128, 1], BF16))
        ones128f = P(nc.sbuf_tensor("ones128f", [128, 64], F32))
        bias_cols = P(nc.sbuf_tensor("bias_cols", [128, NP], F32))
        mur_sb = P(nc.sbuf_tensor("mur_sb", [128, CT], F32))
        muk_sb = P(nc.sbuf_tensor("muk_sb", [128, CT], F32))
        qm_sb = P(nc.sbuf_tensor("qm_sb", [1, 2 * NP * QC], F32))
        qkm_sb = P(nc.sbuf_tensor("qkm_sb", [1, 2 * NP], F32))
        prod_sb = P(nc.sbuf_tensor("prod_sb", [1, NP], F32))
        nmag_sb = P(nc.sbuf_tensor("nmag_sb", [1, NP], F32))
        st_sb = P(nc.sbuf_tensor("st_sb", [128, 12], F32))
        mv_sb = [P(nc.sbuf_tensor(f"mv{i}", [128, 2], F32)) for i in range(4)]
        sc_sb = [P(nc.sbuf_tensor(f"sc{i}", [128, 1], F32)) for i in range(4)]
        ms_sb = P(nc.sbuf_tensor("ms_sb", [128, 4], F32))
        nms_sb = P(nc.sbuf_tensor("nms_sb", [128, 4], F32))
        gcol_sb = P(nc.sbuf_tensor("gcol_sb", [128, CT], F32))
        bcol_sb = P(nc.sbuf_tensor("bcol_sb", [128, CT], F32))
        bcolb_sb = P(nc.sbuf_tensor("bcolb_sb", [128, CT], BF16))
        ceps2_sb = P(nc.sbuf_tensor("ceps2_sb", [128, 1], F32))
        ceps_sb = P(nc.sbuf_tensor("ceps_sb", [128, 1], F32))
        cln8_sb = P(nc.sbuf_tensor("cln8_sb", [1, 1], F32))
        zcol_sb = P(nc.sbuf_tensor("zcol_sb", [128, 1], BF16))
        ch05_sb = P(nc.sbuf_tensor("ch05_sb", [128, 1], F32))

        ph = Emit()
        ph.op("vector", lambda e: e.memset(ones2f[:], 1.0))
        ph.op("vector", lambda e: e.memset(ones1b[:], 1.0))
        ph.op("vector", lambda e: e.memset(ones128b[:], 1.0))
        ph.op("vector", lambda e: e.memset(ones128f[:], 1.0))
        ph.op("vector", lambda e: e.memset(ceps2_sb[:], EPS * EPS))
        ph.op("vector", lambda e: e.memset(ceps_sb[:], EPS))
        ph.op("vector", lambda e: e.memset(zcol_sb[:], 0.0))
        ph.op("vector", lambda e: e.memset(ch05_sb[:], 0.5))
        ph.op("vector", lambda e: e.memset(cln8_sb[:], -LN8), inc=s_v)
        ph.drain("vector")
        ph.op("gpsimd", lambda e: e.dma_start(out=mur_sb[:], in_=mur_h), inc=d_wc, k=16)
        ph.op("gpsimd", lambda e: e.dma_start(out=muk_sb[:], in_=muk_h), inc=d_wc, k=16)

        p2core = E(ExitStack())
        Pc = p2core.enter_context
        kT_sb = Pc(nc.sbuf_tensor("kT_sb", [128, NP * S], BF16))
        ps_sc = Pc(nc.psum_tensor("ps_sc", [128, 2048], F32))
        ps_o = [Pc(nc.psum_tensor(f"ps_o{i}", [128, 512], F32)) for i in range(2)]
        ps_bcj = Pc(nc.psum_tensor("ps_bcj", [128, 512], F32))
        ps_bc = [ps_bcj[0:64, :], ps_bcj[64:128, :]]
        ps_w3 = Pc(nc.psum_tensor("ps_w3", [128, 512], F32))

        r_xat = ExitStack()
        Rx = r_xat.enter_context
        xat_sb = Rx(nc.sbuf_tensor("xat_sb", [128, 2 * CT * 512], BF16, side="right"))
        sq4_sb = Rx(nc.sbuf_tensor("sq4_sb", [128, 4 * 512], BF16, side="right"))
        r_w1 = ExitStack()
        Rw = r_w1.enter_context
        wq_sb = Rw(nc.sbuf_tensor("wq_sb", [128, CT * D], BF16, side="right"))
        wk_sb = Rw(nc.sbuf_tensor("wk_sb", [128, CT * D], BF16, side="right"))
        qstage_sb = [Rw(nc.sbuf_tensor(f"qstage{i}", [128, 512], BF16, side="right")) for i in range(4)]
        bqk_sb = Rw(nc.sbuf_tensor("bqk_sb", [128, 2 * NP], F32, side="right"))
        wv_sb = Rw(nc.sbuf_tensor("wv_sb", [128, CT * D], BF16, side="right"))
        bvr_sb = Rw(nc.sbuf_tensor("bvr_sb", [1, D], BF16, side="right"))
        bvb_sb = Rw(nc.sbuf_tensor("bvb_sb", [128, D], BF16, side="right"))
        r_p1 = ExitStack()
        Rp = r_p1.enter_context
        x_sb = Rp(nc.sbuf_tensor("x_sb", [128, 4 * D], F32, side="right"))
        xa_sb = Rp(nc.sbuf_tensor("xa_sb", [128, 2 * D], BF16, side="right"))

        ph.op("gpsimd", lambda e: e.dma_start(out=gcol_sb[:], in_=gam_h), inc=d_wc, k=16)
        ph.op("gpsimd", lambda e: e.dma_start(out=bcol_sb[:], in_=bet_h), inc=d_wc, k=16)
        consts_loaded = d_wc.n
        for wn, w_sb in (("wq", wq_sb), ("wk", wk_sb)):
            for c in range(CT):
                ph.op("gpsimd", lambda e, o=w_sb[:, c * D:(c + 1) * D],
                      a=w_hbms[wn][c * 128:(c + 1) * 128, :]:
                      e.dma_start(out=o, in_=a), inc=d_w1, k=16)
        w1_loaded = d_w1.n
        for c in range(CT):
            ph.op("gpsimd", lambda e, o=wv_sb[:, c * D:(c + 1) * D],
                  a=w_hbms["wv"][c * 128:(c + 1) * 128, :]:
                  e.dma_start(out=o, in_=a), inc=d_wv, k=16)
        wv_loaded = d_wv.n

        apply_chk = []
        xa_w_chk = []
        fold1 = {}

        def emit_fold1():
            ph.wait("vector", d_wc, consts_loaded)
            ph.op("vector", lambda e: e.tensor_copy(bcolb_sb[:], bcol_sb[:]), inc=s_v)
            bcb_chk = s_v.n
            ph.wait("tensor", s_v, bcb_chk)
            ph.wait("tensor", d_w1, w1_loaded)
            ph.wait("tensor", d_wc, consts_loaded)
            cp_chk = 0
            for wi, wsb in enumerate((wq_sb, wk_sb)):
                for o in range(NP):
                    if cp_chk:
                        ph.wait("tensor", s_v, cp_chk)
                    for c in range(CT):
                        ph.op("tensor", lambda e, w=wsb, c=c, o=o,
                              st=(c == 0), sp=(c == CT - 1):
                              e.matmul(ps_o[0][:, 0:1],
                                       w[:, c * D + o * 128: c * D + o * 128 + 128],
                                       bcolb_sb[:, c:c + 1], start=st, stop=sp),
                              inc=(s_t if c == CT - 1 else None))
                    ph.wait("vector", s_t)
                    ph.op("vector", lambda e, col=wi * NP + o:
                          e.tensor_copy(bqk_sb[:, col:col + 1], ps_o[0][:, 0:1]), inc=s_v)
                    cp_chk = s_v.n
            fold1["cp_vec"] = cp_chk
            fold1["bias_mm_t"] = s_t.n
            ph.wait("vector", s_t, fold1["bias_mm_t"])
            ph.wait("vector", d_w1, w1_loaded)
            ph.wait("vector", d_wc, consts_loaded)
            for wsb in (wq_sb, wk_sb):
                for c in range(CT):
                    ph.op("vector", lambda e, w=wsb, c=c:
                          e.tensor_scalar_mul(w[:, c * D:(c + 1) * D], w[:, c * D:(c + 1) * D],
                                              gcol_sb[:, c:c + 1]),
                          inc=(s_v if (wsb is wk_sb and c == CT - 1) else None))
            fold1["done"] = s_v.n

        for i in range(NT):
            p = i % 4
            xs = x_sb[:, p * D:(p + 1) * D]
            xas = xa_sb[:, (i % 2) * D:(i % 2 + 1) * D]
            if i >= 4:
                ph.wait("sync", s_s, apply_chk[i - 4])
            ph.op("sync", lambda e, o=xs, i=i: e.dma_start(out=o, in_=x_h[i * 128:(i + 1) * 128, :]),
                  inc=d_x[p], k=16)
            x_chk = d_x[p].n
            ph.wait("vector", d_x[p], x_chk)
            if i >= 4:
                ph.wait("vector", s_s, apply_chk[i - 4])
            for half in range(EC):
                ph.op("vector", lambda e, half=half, xs=xs:
                      e.bn_stats(st_sb[:, half * 6:half * 6 + 6], xs[:, half * 512:(half + 1) * 512]))
            ph.drain("vector")
            ph.op("vector", lambda e, p=p: e.bn_aggr(mv_sb[p][:], st_sb[:, 0:6 * EC]), inc=s_v)
            ph.drain("vector")
            ph.wait("scalar", s_v)
            ph.op("scalar", lambda e, p=p:
                  e.activation(sc_sb[p][:], mv_sb[p][:, 1:2], AF.Ln, scale=(1.0 + EPS),
                               bias=ceps2_sb[:]))
            ph.drain("scalar")
            ph.op("scalar", lambda e, p=p:
                  e.activation(sc_sb[p][:], sc_sb[p][:], AF.Exp, scale=-0.5))
            ph.drain("scalar")
            ph.op("scalar", lambda e, p=p:
                  e.activation(ms_sb[:, p:p + 1], mv_sb[p][:, 0:1], AF.Copy, scale=sc_sb[p][:]))
            ph.drain("scalar")
            ph.op("scalar", lambda e, p=p:
                  e.activation(nms_sb[:, p:p + 1], ms_sb[:, p:p + 1], AF.Copy, scale=-1.0))
            ph.drain("scalar")
            if i >= 2:
                ph.wait("scalar", d_xa[(i - 2) % 4], xa_w_chk[i - 2])
            ph.op("scalar", lambda e, xas=xas, xs=xs, p=p:
                  e.activation(xas, xs, AF.Identity, scale=sc_sb[p][:], bias=nms_sb[:, p:p + 1]),
                  inc=s_s)
            apply_chk.append(s_s.n)
            ph.wait("scalar", s_s, apply_chk[i])
            ph.op("scalar", lambda e, xas=xas, i=i:
                  e.dma_start(out=xa_hbm[i * 128:(i + 1) * 128, :], in_=xas), inc=d_xa[i % 4], k=16)
            xa_w_chk.append(d_xa[i % 4].n)
            if i == 8:
                emit_fold1()
        if NT <= 8:
            emit_fold1()
        p1_dxa = tuple(s.n for s in d_xa)
        p1_ss = s_s.n

        ph.wait("tensor", d_wv, wv_loaded)
        ph.wait("tensor", d_wc, consts_loaded)
        ph.wait("tensor", s_v, fold1.get("cp_vec", 0))
        bvcp = 0
        for ec in range(EC):
            if bvcp:
                ph.wait("tensor", s_v, bvcp)
            for c in range(CT):
                ph.op("tensor", lambda e, c=c, ec=ec, st=(c == 0), sp=(c == CT - 1):
                      e.matmul(ps_o[0][0:1, 0:512], bcolb_sb[:, c:c + 1],
                               wv_sb[:, c * D + ec * 512: c * D + ec * 512 + 512],
                               start=st, stop=sp),
                      inc=(s_t if c == CT - 1 else None))
            ph.wait("vector", s_t)
            ph.op("vector", lambda e, ec=ec:
                  e.tensor_copy(bvr_sb[0:1, ec * 512:(ec + 1) * 512], ps_o[0][0:1, 0:512]),
                  inc=s_v)
            bvcp = s_v.n
        fold2_mm_t = s_t.n
        for ec in range(EC):
            ph.wait("tensor", s_v, bvcp)
            ph.op("tensor", lambda e, ec=ec:
                  e.matmul(ps_o[0][:, 0:512], ones1b[:],
                           bvr_sb[0:1, ec * 512:(ec + 1) * 512], start=True, stop=True),
                  inc=s_t)
            ph.wait("vector", s_t)
            ph.op("vector", lambda e, ec=ec:
                  e.tensor_copy(bvb_sb[:, ec * 512:(ec + 1) * 512], ps_o[0][:, 0:512]),
                  inc=s_v)
            bvcp = s_v.n
        ph.wait("vector", s_t, fold2_mm_t)
        for c in range(CT):
            ph.op("vector", lambda e, c=c:
                  e.tensor_scalar_mul(wv_sb[:, c * D:(c + 1) * D], wv_sb[:, c * D:(c + 1) * D],
                                      gcol_sb[:, c:c + 1]),
                  inc=(s_v if c == CT - 1 else None))
        fold2_done = s_v.n

        r_p1.close()
        v_sb = Pc(nc.sbuf_tensor("v_sb", [128, NT * HW], BF16))
        ph.wait("vector", s_s, p1_ss)
        for _i4 in range(4):
            ph.wait("vector", d_x[_i4], d_x[_i4].n)
            ph.wait("vector", d_xa[_i4], p1_dxa[_i4])
        ph.op("vector", lambda e: e.memset(v_sb[:], 1.0), inc=s_v)
        v_ones_done = s_v.n

        pj_uses = []
        nuse = 0
        xat_consumed_chk = {}
        xat_w_chk = {}
        qts_chk = {}

        norm_banks = [ps_w3, ps_bcj]
        norm_st = {"u": 0, "mm_chk": 0, "red_chk": 0, "buf": [], "pend": None}

        def _norm_flush():
            if norm_st["pend"] is None:
                return
            buf, sl, sq_done = norm_st["pend"]
            norm_st["pend"] = None
            ph.wait("tensor", s_v, sq_done)
            if norm_st["red_chk"]:
                ph.wait("tensor", s_v, norm_st["red_chk"])
            for j2 in range(2):
                ph.op("tensor", lambda e, j2=j2, sl=sl:
                      e.matmul(norm_banks[j2][0:1, 0:512], ones128b[:],
                               sq4_sb[:, (sl + j2) * 512:(sl + j2 + 1) * 512],
                               start=True, stop=True),
                      inc=(s_t if j2 == 1 else None))
            norm_st["mm_chk"] = s_t.n
            ph.wait("vector", s_t, norm_st["mm_chk"])
            for j2, (idx, dst) in enumerate(buf):
                ph.op("vector", lambda e, j2=j2, idx=idx:
                      e.reduce_max(qm_sb[0:1, idx:idx + 1],
                                   norm_banks[j2][0:1, 0:512], axis=AX.X),
                      inc=(s_v if j2 == 1 else None))
            norm_st["red_chk"] = s_v.n

        def emit_norm(idx_qm, dst_ap):
            norm_st["buf"].append((idx_qm, dst_ap))
            norm_st["u"] += 1
            if norm_st["u"] % 2 == 0:
                g = norm_st["u"] // 2 - 1
                sl = (g % 2) * 2
                _fs, _fc = pj_uses[-1]
                ph.wait("vector", _fs, _fc)
                if norm_st["mm_chk"]:
                    ph.wait("vector", s_t, norm_st["mm_chk"])
                for j2, (idx, dst) in enumerate(norm_st["buf"]):
                    ph.op("vector", lambda e, j2=j2, dst=dst, sl=sl:
                          e.tensor_mul(sq4_sb[:, (sl + j2) * 512:(sl + j2 + 1) * 512],
                                       dst, dst),
                          inc=(s_v if j2 == 1 else None))
                sq_done = s_v.n
                _norm_flush()
                norm_st["pend"] = (norm_st["buf"], sl, sq_done)
                norm_st["buf"] = []

        for ch in range(QC):
            pxt = ch % 2
            if ch >= 2:
                ph.wait("sync", s_pjt, xat_consumed_chk[ch - 2])
            last_tile = ch * 4 + 3
            for t_ in range(max(0, last_tile - 3), last_tile + 1):
                ph.wait("sync", d_xa[t_ % 4], xa_w_chk[t_])
            for c in range(CT):
                ph.op("sync", lambda e, o=xat_sb[:, (pxt * CT + c) * 512:(pxt * CT + c + 1) * 512],
                      a=xa_hbm[ch * 512:(ch + 1) * 512, c * 128:(c + 1) * 128]:
                      e.dma_start_transpose(out=o, in_=a), inc=d_xat[pxt], k=16)
            xat_w_chk[ch] = d_xat[pxt].n
            ph.wait("tensor", d_xat[pxt], xat_w_chk[ch])
            ph.wait("tensor", d_w1, w1_loaded)
            ph.wait("tensor", s_v, fold1["done"])
            for (wsb, is_q) in ((wq_sb, True), (wk_sb, False)):
                for o in range(NP):
                    bank = nuse % 4
                    if nuse >= 4:
                        _fs, _fc = pj_uses[nuse - 4]
                        ph.wait("tensor", _fs, _fc)
                    for c in range(CT):
                        ph.op("tensor", lambda e, b=bank, w=wsb, c=c, o=o, pxt=pxt,
                              st=(c == 0), sp=(c == CT - 1):
                              e.matmul(ps_sc[:, b * 512:(b + 1) * 512],
                                       w[:, c * D + o * 128: c * D + o * 128 + 128],
                                       xat_sb[:, (pxt * CT + c) * 512:(pxt * CT + c + 1) * 512],
                                       start=st, stop=sp),
                              inc=(s_pjt if c == CT - 1 else None))
                    ph.wait("scalar", s_pjt)
                    if is_q:
                        u = len(qts_chk)
                        slot = u % 4
                        if u >= 4:
                            ph.wait("scalar", d_qts[slot], qts_chk[u - 4])
                        ph.op("scalar", lambda e, b=bank, slot=slot, o=o:
                              e.activation(qstage_sb[slot][:],
                                           ps_sc[:, b * 512:(b + 1) * 512], AF.Identity,
                                           bias=bqk_sb[:, o:o + 1]),
                              inc=s_pjv)
                        pj_uses.append((s_pjv, s_pjv.n))
                        emit_norm(o * QC + ch, qstage_sb[slot][:])
                        ph.wait("gpsimd", s_pjv, s_pjv.n)
                        ph.op("gpsimd", lambda e, slot=slot, o=o, ch=ch:
                              e.dma_start(out=qt_hbm[o * 128:(o + 1) * 128, ch * 512:(ch + 1) * 512],
                                          in_=qstage_sb[slot][:]), inc=d_qts[slot], k=16)
                        qts_chk[u] = d_qts[slot].n
                    else:
                        dst = kT_sb[:, o * S + ch * 512: o * S + (ch + 1) * 512]
                        ph.op("scalar", lambda e, b=bank, dst=dst, o=o:
                              e.activation(dst, ps_sc[:, b * 512:(b + 1) * 512], AF.Identity,
                                           bias=bqk_sb[:, NP + o:NP + o + 1]),
                              inc=s_pjv)
                        pj_uses.append((s_pjv, s_pjv.n))
                        emit_norm(NP * QC + o * QC + ch, dst)
                    nuse += 1
            ph.wait("tensor", s_v, fold2_done)
            if ch == 0:
                ph.wait("tensor", s_v, v_ones_done)
            for it in range(4):
                for ec in range(EC):
                    bank = nuse % 4
                    if nuse >= 4:
                        _fs, _fc = pj_uses[nuse - 4]
                        ph.wait("tensor", _fs, _fc)
                    for c in range(CT):
                        ph.op("tensor", lambda e, b=bank, c=c, it=it, ec=ec, pxt=pxt,
                              st=(c == 0), sp=(c == CT - 1):
                              e.matmul(ps_sc[:, b * 512:(b + 1) * 512],
                                       xat_sb[:, (pxt * CT + c) * 512 + it * 128:
                                              (pxt * CT + c) * 512 + it * 128 + 128],
                                       wv_sb[:, c * D + ec * 512: c * D + ec * 512 + 512],
                                       start=st, stop=sp),
                              inc=(s_pjt if c == CT - 1 else None))
                    ph.wait("vector", s_pjt)
                    itile = ch * 4 + it
                    hh0 = ec * (512 // dh)
                    ph.op("vector", lambda e, b=bank, itile=itile, hh0=hh0, ec=ec:
                          e.tensor_tensor(
                              v_sb[:, itile * HW + hh0 * 65: itile * HW + (hh0 + 8) * 65]
                              .rearrange("p (h ee) -> p h ee", ee=65)[:, :, 0:64],
                              ps_sc[:, b * 512:(b + 1) * 512]
                              .rearrange("p (h ee) -> p h ee", ee=64),
                              bvb_sb[:, ec * 512:(ec + 1) * 512]
                              .rearrange("p (h ee) -> p h ee", ee=64),
                              ALU.add),
                          inc=s_pjw)
                    pj_uses.append((s_pjw, s_pjw.n))
                    nuse += 1
            xat_consumed_chk[ch] = s_pjt.n
        pass1_pjt = s_pjt.n
        qkv_evac_done = s_pjv.n
        qkv_evw_done = s_pjw.n
        qkv_mm_done = s_pjt.n

        _norm_flush()
        ph.drain("vector")
        for j2 in range(2 * NP):
            ph.op("vector", lambda e, j2=j2:
                  e.reduce_max(qkm_sb[0:1, j2:j2 + 1], qm_sb[0:1, j2 * QC:(j2 + 1) * QC],
                               axis=AX.X),
                  inc=(s_v if j2 == 2 * NP - 1 else None))
        ph.drain("vector")
        ph.op("vector", lambda e: e.tensor_tensor(
            prod_sb[0:1, :], qkm_sb[0:1, 0:NP], qkm_sb[0:1, NP:2 * NP], ALU.mult), inc=s_v)
        ph.drain("vector")
        ph.wait("scalar", s_v)
        ph.op("scalar", lambda e: e.activation(nmag_sb[0:1, :], prod_sb[0:1, :], AF.Ln))
        ph.drain("scalar")
        ph.op("scalar", lambda e: e.activation(nmag_sb[0:1, :], nmag_sb[0:1, :], AF.Exp,
                                               scale=0.5, bias=cln8_sb[:]), inc=s_s)
        ph.wait("vector", s_s)
        ph.op("vector", lambda e: e.tensor_scalar_mul(nmag_sb[0:1, :], nmag_sb[0:1, :], -1.0),
              inc=s_v)
        nmag_chk = s_v.n
        bias_copy_chk = 0
        for j in range(NP):
            ph.wait("tensor", s_v, nmag_chk)
            if j > 0:
                ph.wait("tensor", s_v, bias_copy_chk)
            ph.op("tensor", lambda e, j=j:
                  e.matmul(ps_o[0][0:128, 0:1], ones2f[0:1, :], nmag_sb[0:1, j:j + 1],
                           start=True, stop=True), inc=s_t)
            ph.wait("vector", s_t)
            ph.op("vector", lambda e, j=j:
                  e.tensor_copy(bias_cols[:, j:j + 1], ps_o[0][0:128, 0:1]), inc=s_v)
            bias_copy_chk = s_v.n
        bias_ready = s_v.n

        r_w1.close()
        r_xat.close()

        exp_sb = Pc(nc.sbuf_tensor("exp_sb", [128, 3072], BF16))
        qtc_sb = Pc(nc.sbuf_tensor("qtc_sb", [128, NP * 512], BF16))
        stage_sb = [Pc(nc.sbuf_tensor(f"stage{i}", [128, 512], F32)) for i in range(4)]
        attnb_sb = [Pc(nc.sbuf_tensor(f"attnb{i}", [64, 512], BF16)) for i in range(2)]

        p3sh = ExitStack()
        Ps = p3sh.enter_context
        st3_sb = [Ps(nc.sbuf_tensor(f"st3_{i}", [128, 12], F32, side="right")) for i in range(2)]
        mv3_sb = [Ps(nc.sbuf_tensor(f"mv3_{i}", [128, 4], F32, side="right")) for i in range(2)]
        rs3_sb = [Ps(nc.sbuf_tensor(f"rs3_{i}", [128, 4], F32, side="right")) for i in range(2)]
        p3a = ExitStack()
        Pa = p3a.enter_context
        wo_sb = Pa(nc.sbuf_tensor("wo_sb", [128, CT * D], BF16, side="right"))
        at3_sb = Pa(nc.sbuf_tensor("at3_sb", [128, CT * 512], BF16, side="right"))
        x3_sb = Pa(nc.sbuf_tensor("x3_sb", [128, 2 * D], F32, side="right"))
        xn_sb = Pa(nc.sbuf_tensor("xn_sb", [128, 2 * D], BF16, side="right"))
        x1r_sb = Pa(nc.sbuf_tensor("x1r_sb", [128, 3 * D], F32, side="right"))
        wo3_sb = Pa(nc.sbuf_tensor("wo3_sb", [128, D], F32, side="right"))

        ph.wait("gpsimd", s_pjt, qkv_mm_done)
        ph.wait("gpsimd", s_pjv, qkv_evac_done)
        ph.wait("gpsimd", s_pjw, qkv_evw_done)
        ph.wait("gpsimd", s_t, s_t.n)
        ph.wait("gpsimd", s_v, s_v.n)
        ph.wait("gpsimd", d_xat[0], d_xat[0].n)
        ph.wait("gpsimd", d_xat[1], d_xat[1].n)
        for c in range(CT):
            ph.op("gpsimd", lambda e, o=wo_sb[:, c * D:(c + 1) * D],
                  a=w_hbms["wo"][c * 128:(c + 1) * 128, :]:
                  e.dma_start(out=o, in_=a), inc=d_wo, k=16)
        wo_loaded = d_wo.n
        ph.wait("vector", s_pjv, qkv_evac_done)
        ph.wait("vector", s_pjw, qkv_evw_done)
        ph.wait("vector", s_pjt, qkv_mm_done)
        ph.wait("vector", d_wv, wv_loaded)
        ph.wait("vector", d_xat[0], d_xat[0].n)
        ph.wait("vector", d_xat[1], d_xat[1].n)
        ph.op("vector", lambda e: e.memset(xn_sb[0:16, 0:D], 0.0), inc=s_v)
        z16_chk = s_v.n
        ph.wait("gpsimd", s_v, z16_chk)
        ph.op("gpsimd", lambda e: e.dma_start(out=xn_hbm[0:16, :], in_=xn_sb[0:16, 0:D]),
              inc=d_xnw[0], k=16)
        z16_dma = d_xnw[0].n

        attn_iter = 0
        pending = [None]
        tail_mm2 = [None]
        prev_chk = None
        attn_free_chk = {}
        ps_o_free_chk = {}
        attn_w_chk = {}
        attn_spill_after_qc = {}
        qtr_chk = {}
        sct_before_qc = {}

        wo_t = {}
        woc_chk = {}
        a3c = {}
        r3c = {}
        x1c = {}
        a4c = {}
        r4c = {}
        xnc = {}
        x1w_chk = {}
        xnw_chk = {}
        at3_chk = {}
        x3_chk = {}

        def p3a_loads(ch):
            d0, d1 = attn_spill_after_qc[ch]
            ph.wait("sync", d_attn[0], d0)
            ph.wait("sync", d_attn[1], d1)
            if ch >= 1:
                ph.wait("sync", s_t, wo_t[(ch - 1) * 4 + 3])
            for c in range(CT):
                ph.op("sync", lambda e, o=at3_sb[:, c * 512:(c + 1) * 512],
                      a=attn_hbm[c * 128:(c + 1) * 128, ch * 512:(ch + 1) * 512]:
                      e.dma_start(out=o, in_=a), inc=d_at3[ch % 2], k=16)
            at3_chk[ch] = d_at3[ch % 2].n

        def p3a_x3load(ti):
            if ti >= 2:
                ph.wait("sync", s_v, x1c[ti - 2])
            ph.op("sync", lambda e, ti=ti, sl=(ti % 2) * D:
                  e.dma_start(out=x3_sb[:, sl:sl + D],
                              in_=x_h[ti * 128:(ti + 1) * 128, :]),
                  inc=d_x3[ti % 2], k=16)
            x3_chk[ti] = d_x3[ti % 2].n

        def p3a_T_ec(ti, ec):
            ch, it = ti // 4, ti % 4
            if ec == 0:
                ph.wait("tensor", d_at3[ch % 2], at3_chk[ch])
                ph.wait("tensor", d_wo, wo_loaded)
            if ti > 0 or ec > 0:
                ph.wait("tensor", s_v,
                        woc_chk[(ti, ec - 1) if ec > 0 else (ti - 1, EC - 1)])
            for c in range(CT):
                ph.op("tensor", lambda e, c=c, it=it, ec=ec,
                      st=(c == 0), sp=(c == CT - 1):
                      e.matmul(ps_w3[:],
                               at3_sb[:, c * 512 + it * 128: c * 512 + it * 128 + 128],
                               wo_sb[:, c * D + ec * 512: c * D + ec * 512 + 512],
                               start=st, stop=sp),
                      inc=(s_t if c == CT - 1 else None))
            wo_t[ti] = s_t.n
            ph.wait("vector", s_t, wo_t[ti])
            ph.op("vector", lambda e, ec=ec:
                  e.tensor_copy(wo3_sb[:, ec * 512:(ec + 1) * 512], ps_w3[:]), inc=s_v)
            woc_chk[(ti, ec)] = s_v.n

        def p3a_T(ti):
            for ec in range(EC):
                p3a_T_ec(ti, ec)

        def p3a_Av(ti):
            ppw = ti % 2
            ph.drain("vector")
            for half in range(EC):
                ph.op("vector", lambda e, half=half, ppw=ppw:
                      e.bn_stats(st3_sb[ppw][:, half * 6:half * 6 + 6],
                                 wo3_sb[:, half * 512:(half + 1) * 512]))
                ph.drain("vector")
            ph.op("vector", lambda e, ppw=ppw:
                  e.bn_aggr(mv3_sb[ppw][:, 0:2], st3_sb[ppw][:, 0:6 * EC]), inc=s_v)
            a3c[ti] = s_v.n

        def p3a_As(ti):
            ppw = ti % 2
            ph.wait("scalar", s_v, a3c[ti])
            ph.op("scalar", lambda e, ppw=ppw:
                  e.activation(rs3_sb[ppw][:, 0:1], mv3_sb[ppw][:, 1:2], AF.Ln, bias=ceps_sb[:]))
            ph.drain("scalar")
            ph.op("scalar", lambda e, ppw=ppw:
                  e.activation(rs3_sb[ppw][:, 0:1], rs3_sb[ppw][:, 0:1], AF.Exp, scale=-0.5),
                  inc=s_s)
            r3c[ti] = s_s.n

        def p3a_Bv(ti):
            ppw = ti % 2
            off = (ti % 3) * D
            x3o = (ti % 2) * D
            ph.wait("vector", s_s, r3c[ti])
            ph.wait("vector", d_x3[ti % 2], x3_chk[ti])
            if ti >= 3:
                ph.wait("vector", d_x1w[(ti - 3) % 4], x1w_chk[ti - 3])
            ph.op("vector", lambda e, ppw=ppw, off=off:
                  e.tensor_scalar(x1r_sb[:, off:off + D], wo3_sb[:, 0:D],
                                  mv3_sb[ppw][:, 0:1], rs3_sb[ppw][:, 0:1], ALU.subtract, ALU.mult))
            ph.drain("vector")
            ph.op("vector", lambda e, off=off, x3o=x3o:
                  e.tensor_tensor(x1r_sb[:, off:off + D], x1r_sb[:, off:off + D],
                                  x3_sb[:, x3o:x3o + D], ALU.add), inc=s_v)
            x1c[ti] = s_v.n
            ph.drain("vector")
            for half in range(EC):
                ph.op("vector", lambda e, half=half, off=off, ppw=ppw:
                      e.bn_stats(st3_sb[ppw][:, half * 6:half * 6 + 6],
                                 x1r_sb[:, off + half * 512:off + (half + 1) * 512]))
                ph.drain("vector")
            ph.op("vector", lambda e, ppw=ppw:
                  e.bn_aggr(mv3_sb[ppw][:, 2:4], st3_sb[ppw][:, 0:6 * EC]), inc=s_v)
            a4c[ti] = s_v.n

        def p3a_Bs(ti):
            ppw = ti % 2
            ph.wait("scalar", s_v, a4c[ti])
            ph.op("scalar", lambda e, ppw=ppw:
                  e.activation(rs3_sb[ppw][:, 1:2], mv3_sb[ppw][:, 3:4], AF.Ln, bias=ceps_sb[:]))
            ph.drain("scalar")
            ph.op("scalar", lambda e, ppw=ppw:
                  e.activation(rs3_sb[ppw][:, 1:2], rs3_sb[ppw][:, 1:2], AF.Exp, scale=-0.5),
                  inc=s_s)
            r4c[ti] = s_s.n

        def p3a_C(ti):
            ppw = ti % 2
            off = (ti % 3) * D
            xno = (ti % 2) * D
            ph.wait("vector", s_s, r4c[ti])
            if ti == 0:
                ph.wait("vector", d_xnw[0], z16_dma)
            if ti >= 2:
                ph.wait("vector", d_xnw[ti % 2], xnw_chk[ti - 2])
            ph.op("vector", lambda e, off=off, xno=xno, ppw=ppw:
                  e.tensor_scalar(xn_sb[:, xno:xno + D], x1r_sb[:, off:off + D],
                                  mv3_sb[ppw][:, 2:3], rs3_sb[ppw][:, 1:2], ALU.subtract, ALU.mult),
                  inc=s_v)
            xnc[ti] = s_v.n
            ph.wait("gpsimd", s_v, x1c[ti])
            ph.op("gpsimd", lambda e, ti=ti, off=off:
                  e.dma_start(out=x1_hbm[ti * 128:(ti + 1) * 128, :],
                              in_=x1r_sb[:, off:off + D]),
                  inc=d_x1w[ti % 4], k=16)
            x1w_chk[ti] = d_x1w[ti % 4].n
            ph.wait("gpsimd", s_v, xnc[ti])
            ph.op("gpsimd", lambda e, ti=ti, xno=xno:
                  e.dma_start(out=xn_hbm[16 + ti * 128:16 + (ti + 1) * 128, :],
                              in_=xn_sb[:, xno:xno + D]),
                  inc=d_xnw[ti % 2], k=16)
            xnw_chk[ti] = d_xnw[ti % 2].n

        def p3a_stepT(ti):
            p3a_x3load(ti)
            p3a_T(ti)
            p3a_Av(ti)

        NPH = NP // 2
        qtr_lo_chk = {}
        qtr_hi_chk = {}

        def qtc_load(qc, half):
            o0 = half * NPH
            ph.op("sync", lambda e, qc=qc, o0=o0:
                  e.dma_start(out=qtc_sb[:, o0 * 512:(o0 + NPH) * 512]
                              .rearrange("p (o c) -> p o c", c=512),
                              in_=qt_hbm[o0 * 128:(o0 + NPH) * 128, qc * 512:(qc + 1) * 512]
                              .rearrange("(o p) c -> p o c", p=128)),
                  inc=d_qtr[half], k=16)
            (qtr_lo_chk if half == 0 else qtr_hi_chk)[qc] = d_qtr[half].n

        for qc in range(QC):
            if qc == 0:
                ph.wait("sync", s_pjt, qkv_mm_done)
                ph.wait("sync", s_pjv, qkv_evac_done)
                ph.wait("sync", s_pjw, qkv_evw_done)
                for _s in range(4):
                    ph.wait("sync", d_qts[_s], d_qts[_s].n)
                qtc_load(0, 0)
                qtc_load(0, 1)
            else:
                ph.wait("sync", s_sct, sct_before_qc[qc])
                qtc_load(qc, 1)
            for pj in range(NP):
                idx = attn_iter
                pi = idx % 2
                if pj == NPH:
                    sct_half_mark = s_sct.n
                chk_kt = {}
                for kt in range(KT):
                    bp = (kt % 2) * 2
                    if kt >= 2:
                        ph.wait("tensor", s_exp, chk_kt[kt - 2])
                    elif prev_chk is not None:
                        ph.wait("tensor", s_exp, prev_chk[KT - 2 + kt])
                    if kt == 0:
                        ph.wait("tensor", s_v, bias_ready)
                        if qc == 0 and pj == 0:
                            ph.wait("tensor", s_pjv, qkv_evac_done)
                            ph.wait("tensor", s_pjw, qkv_evw_done)
                        if pj == 0:
                            ph.wait("tensor", d_qtr[0], qtr_lo_chk[qc])
                        elif pj == NPH:
                            ph.wait("tensor", d_qtr[1], qtr_hi_chk[qc])
                    for r2 in range(2):
                        lo = r2 * 64
                        ph.op("tensor", lambda e, bp=bp, pj=pj, lo=lo, kt=kt, r2=r2:
                              e.matmul(ps_sc[:, (bp + r2) * 512:(bp + r2 + 1) * 512],
                                       kT_sb[lo:lo + 64, pj * S + kt * 128: pj * S + (kt + 1) * 128],
                                       qtc_sb[lo:lo + 64, pj * 512:(pj + 1) * 512],
                                       start=True, stop=True),
                              inc=(s_sct if r2 == 1 else None))
                    sct_chk = s_sct.n
                    if kt == 0 and tail_mm2[0] is not None:
                        tail_mm2[0]()
                        tail_mm2[0] = None
                    if kt == min(8, KT - 1):
                        if pending[0] is not None:
                            pending[0]()
                            pending[0] = None
                        if pj == NPH and qc + 1 < QC:
                            ph.wait("sync", s_sct, sct_half_mark)
                            qtc_load(qc + 1, 0)
                        if qc >= 1:
                            t0 = 4 * (qc - 1)
                            e3 = t0 - 1
                            if NP >= 8:
                                if pj == 0:
                                    if qc >= 2:
                                        p3a_As(e3)
                                        p3a_Bv(e3)
                                    p3a_loads(qc - 1)
                                elif pj == 1:
                                    if qc >= 2:
                                        p3a_Bs(e3)
                                        p3a_C(e3)
                                    p3a_stepT(t0)
                                elif pj in (2, 4, 6):
                                    k = pj // 2 - 1
                                    p3a_As(t0 + k)
                                    p3a_Bv(t0 + k)
                                elif pj in (3, 5, 7):
                                    k = (pj - 3) // 2
                                    p3a_Bs(t0 + k)
                                    p3a_C(t0 + k)
                                    if k < 2:
                                        p3a_stepT(t0 + k + 1)
                            else:
                                if pj == 0:
                                    if qc >= 2:
                                        p3a_As(e3)
                                        p3a_Bv(e3)
                                    p3a_loads(qc - 1)
                                    if qc >= 2:
                                        p3a_Bs(e3)
                                        p3a_C(e3)
                                    p3a_stepT(t0)
                                else:
                                    p3a_As(t0 + pj - 1)
                                    p3a_Bv(t0 + pj - 1)
                                    p3a_Bs(t0 + pj - 1)
                                    p3a_C(t0 + pj - 1)
                                    if pj < NP - 1:
                                        p3a_stepT(t0 + pj)
                    if kt >= 2:
                        ktg = kt - 2
                        if kt == 2 and idx >= 1:
                            ph.wait("tensor", s_aev, ps_o_free_chk[idx - 1])
                        eo = (ktg % 3) * 1024
                        for r2 in range(2):
                            hh = 2 * pj + r2
                            ph.op("tensor", lambda e, ktg=ktg, hh=hh, r2=r2, eo=eo,
                                  st=(kt == 2):
                                  e.matmul(ps_o[r2][0:65, :],
                                           v_sb[:, ktg * HW + hh * 65: ktg * HW + hh * 65 + 65],
                                           exp_sb[:, eo + r2 * 512: eo + r2 * 512 + 512],
                                           start=st, stop=False))
                    ph.wait("scalar", s_sct, sct_chk)
                    if kt == 0:
                        ph.wait("scalar", s_v, bias_ready)
                        if idx >= 1:
                            ph.wait("scalar", s_mm2, 2 * idx - 1)
                    if kt == 1 and idx >= 1:
                        ph.wait("scalar", s_mm2, 2 * idx)
                    ph.op("scalar", lambda e, bp=bp, pj=pj, eo=(kt % 3) * 1024:
                          e.activation(exp_sb[:, eo:eo + 1024],
                                       ps_sc[:, bp * 512:(bp + 2) * 512],
                                       AF.Exp, bias=bias_cols[:, pj:pj + 1], scale=0.125),
                          inc=s_exp)
                    chk_kt[kt] = s_exp.n

                def mk_tail(chk30=chk_kt[KT - 2], chk31=chk_kt[KT - 1], pj=pj):
                    def emit():
                        for ktg in (KT - 2, KT - 1):
                            ph.wait("tensor", s_exp, chk30 if ktg == KT - 2 else chk31)
                            eo = (ktg % 3) * 1024
                            for r2 in range(2):
                                hh = 2 * pj + r2
                                ph.op("tensor", lambda e, ktg=ktg, hh=hh, r2=r2, eo=eo,
                                      sp=(ktg == KT - 1):
                                      e.matmul(ps_o[r2][0:65, :],
                                               v_sb[:, ktg * HW + hh * 65: ktg * HW + hh * 65 + 65],
                                               exp_sb[:, eo + r2 * 512: eo + r2 * 512 + 512],
                                               start=False, stop=sp),
                                      inc=(s_mm2 if r2 == 1 else None))
                    return emit
                tail_mm2[0] = mk_tail()
                prev_chk = chk_kt
                mm2_chk = s_mm2.n + 2
                ph.wait("vector", s_mm2, mm2_chk)
                for r2 in range(2):
                    ph.op("vector", lambda e, si=pi * 2 + r2, r2=r2:
                          e.tensor_copy(stage_sb[si][0:65, :], ps_o[r2][0:65, :]), inc=s_aev)
                ps_o_free_chk[idx] = s_aev.n
                ph.drain("vector")
                if pj == NP - 1:
                    ph.wait("scalar", s_aev, ps_o_free_chk[idx])
                    for r2 in range(2):
                        ph.op("scalar", lambda e, si=pi * 2 + r2:
                              e.activation(stage_sb[si][64:65, :], stage_sb[si][64:65, :],
                                           AF.Ln))
                    ph.drain("scalar")
                    for r2 in range(2):
                        ph.op("scalar", lambda e, si=pi * 2 + r2:
                              e.activation(stage_sb[si][64:65, :], stage_sb[si][64:65, :],
                                           AF.Exp, scale=-1.0),
                              inc=s_rec)
                else:
                    for r2 in range(2):
                        ph.op("vector", lambda e, si=pi * 2 + r2:
                              e.reciprocal(stage_sb[si][64:65, :], stage_sb[si][64:65, :]),
                              inc=s_rec)
                rec_chk = s_rec.n

                def mk_pending(rec_chk=rec_chk, pi=pi, idx=idx):
                    def emit():
                        ph.wait("tensor", s_rec, rec_chk)
                        if idx >= 1:
                            ph.wait("tensor", s_amu, 2 * idx)
                        for r2 in range(2):
                            ph.op("tensor", lambda e, si=pi * 2 + r2, r2=r2:
                                  e.matmul(ps_bc[r2], ones128f[64:65, :],
                                           stage_sb[si][64:65, :],
                                           start=True, stop=True,
                                           tile_position=(64, 64 * r2)), inc=s_bct)
                    return emit
                pending[0] = mk_pending()
                ph.wait("vector", s_bct, 2 * (idx + 1))
                if idx >= 1:
                    ph.wait("vector", d_attn[(idx - 1) % 2], attn_w_chk[idx - 1])
                for r2 in range(2):
                    ph.op("vector", lambda e, si=pi * 2 + r2, r2=r2:
                          e.tensor_tensor(attnb_sb[r2][:], stage_sb[si][0:64, :], ps_bc[r2],
                                          ALU.mult), inc=s_amu)
                attn_free_chk[idx] = s_amu.n
                ph.wait("sync", s_amu, attn_free_chk[idx])
                for r2 in range(2):
                    hg_ = 2 * pj + r2
                    ph.op("sync", lambda e, r2=r2, hg_=hg_, qc=qc:
                          e.dma_start(out=attn_hbm[hg_ * 64:(hg_ + 1) * 64, qc * 512:(qc + 1) * 512],
                                      in_=attnb_sb[r2][:]), inc=d_attn[pi], k=16)
                attn_w_chk[idx] = d_attn[pi].n
                attn_iter += 1
            if qc >= 1:
                t3b = 4 * (qc - 1) + 3
                p3a_x3load(t3b)
                p3a_T_ec(t3b, 0)
            if tail_mm2[0] is not None:
                tail_mm2[0]()
                tail_mm2[0] = None
            if pending[0] is not None:
                pending[0]()
                pending[0] = None
            if qc >= 1:
                for _ec in range(1, EC):
                    p3a_T_ec(t3b, _ec)
                p3a_Av(t3b)
            attn_spill_after_qc[qc] = (d_attn[0].n, d_attn[1].n)
            sct_before_qc[qc + 1] = s_sct.n
        if tail_mm2[0] is not None:
            tail_mm2[0]()
            tail_mm2[0] = None
        if pending[0] is not None:
            pending[0]()
            pending[0] = None
        NTI = 4 * QC
        if QC >= 2:
            p3a_As(4 * (QC - 2) + 3)
            p3a_Bv(4 * (QC - 2) + 3)
        p3a_loads(QC - 1)
        if QC >= 2:
            p3a_Bs(4 * (QC - 2) + 3)
            p3a_C(4 * (QC - 2) + 3)
        for ti in range(4 * (QC - 1), NTI):
            p3a_stepT(ti)
            p3a_As(ti)
            p3a_Bv(ti)
            p3a_Bs(ti)
            p3a_C(ti)
        p3a_xnw = (d_xnw[0].n, d_xnw[1].n)
        p2_attn_dma = (d_attn[0].n, d_attn[1].n)
        p2_exp_done = s_exp.n

        p3a.close()
        p2core.close()
        p3c = ExitStack()
        P = p3c.enter_context
        wrt_sb = P(nc.sbuf_tensor("wrt_sb", [128, CT * D], BF16))
        wkt_sb = P(nc.sbuf_tensor("wkt_sb", [128, CT * D], BF16))
        wvt_sb = P(nc.sbuf_tensor("wvt_sb", [128, CT * D], BF16))
        ph.wait("gpsimd", s_exp, p2_exp_done)
        ph.wait("gpsimd", s_amu, s_amu.n)
        ph.wait("gpsimd", s_mm2, s_mm2.n)
        ph.wait("gpsimd", s_v, s_v.n)
        ph.wait("gpsimd", s_t, s_t.n)
        ph.wait("gpsimd", s_s, s_s.n)
        ph.wait("gpsimd", d_attn[0], p2_attn_dma[0])
        ph.wait("gpsimd", d_attn[1], p2_attn_dma[1])
        ph.wait("gpsimd", d_qtr[0], d_qtr[0].n)
        ph.wait("gpsimd", d_qtr[1], d_qtr[1].n)
        ph.wait("gpsimd", d_at3[0], d_at3[0].n)
        ph.wait("gpsimd", d_at3[1], d_at3[1].n)
        ph.wait("gpsimd", d_x3[0], d_x3[0].n)
        ph.wait("gpsimd", d_x3[1], d_x3[1].n)
        for _s in range(4):
            ph.wait("gpsimd", d_x1w[_s], d_x1w[_s].n)
        ph.wait("gpsimd", d_xnw[0], p3a_xnw[0])
        ph.wait("gpsimd", d_xnw[1], p3a_xnw[1])
        for wn, w_sb in (("wrt", wrt_sb), ("wkt", wkt_sb), ("wvt", wvt_sb)):
            for c in range(CT):
                ph.op("gpsimd", lambda e, o=w_sb[:, c * D:(c + 1) * D],
                      a=w_hbms[wn][c * 128:(c + 1) * 128, :]:
                      e.dma_start(out=o, in_=a), inc=d_wf, k=16)
        w_p3c_loaded = d_wf.n
        xnT_sb = P(nc.sbuf_tensor("xnT_sb", [128, 2 * CT * 512], BF16))
        xrT_sb = P(nc.sbuf_tensor("xrT_sb", [128, 2 * CT * 512], BF16))
        xkT_sb = P(nc.sbuf_tensor("xkT_sb", [128, 2 * CT * 512], BF16))
        dmix_sb = P(nc.sbuf_tensor("dmix_sb", [128, 2 * 512], BF16))
        kact_sb = P(nc.sbuf_tensor("kact_sb", [128, CT * 512], BF16))
        k1_sb = P(nc.sbuf_tensor("k1_sb", [128, 2 * 512], BF16))
        rt_sb = P(nc.sbuf_tensor("rt_sb", [128, 2 * 4 * D], BF16))
        g1_sb = P(nc.sbuf_tensor("g1_sb", [128, D], BF16))
        gg_sb = P(nc.sbuf_tensor("gg_sb", [128, 2 * D], F32))
        ob_sb = P(nc.sbuf_tensor("ob_sb", [128, 2 * 4 * D], F32))
        x1c_sb = P(nc.sbuf_tensor("x1c_sb", [128, 4 * D], F32))
        ps_k3 = [P(nc.psum_tensor(f"ps_k3{i}", [128, 512], F32)) for i in range(2)]
        ps_wo = [P(nc.psum_tensor(f"ps_wo{i}", [128, 1024], F32)) for i in range(3)]
        pwo_use = [0]
        pwo_free = {}
        pwo_bank = {}

        sync_p3c_barrier = [False]

        mix_done = {}
        rk_done = {}
        out_done = {}
        rtanh_chk = {}
        krelu_chk = {}
        ksq_chk = {}
        v3_chk = {}
        tr_chk = {}
        x1r_chk2 = {}
        out_w_chk = {}

        def p3c_trloads(ch):
            pp = ch % 2
            if not sync_p3c_barrier[0]:
                sync_p3c_barrier[0] = True
                ph.wait("sync", s_amu, s_amu.n)
                ph.wait("sync", s_exp, p2_exp_done)
                ph.wait("sync", d_xnw[0], p3a_xnw[0])
                ph.wait("sync", d_xnw[1], p3a_xnw[1])
            if ch >= 2:
                ph.wait("sync", s_v, mix_done[ch - 1])
            for c in range(CT):
                ph.op("sync", lambda e, o=xnT_sb[:, (pp * CT + c) * 512:(pp * CT + c + 1) * 512],
                      a=xn_hbm[16 + ch * 512:16 + (ch + 1) * 512, c * 128:(c + 1) * 128]:
                      e.dma_start_transpose(out=o, in_=a), inc=d_tr3[pp], k=16)
            tr_chk[ch] = d_tr3[pp].n

        def p3c_x1load(ch):
            pp = ch % 2
            if ch >= 1:
                ph.wait("gpsimd", s_v, out_done[ch - 1])
            for t in range(ch * 4, ch * 4 + 4):
                ph.wait("gpsimd", d_x1w[t % 4], x1w_chk[t])
            ph.op("gpsimd", lambda e, ch=ch:
                  e.dma_start(out=x1c_sb[:].rearrange("p (i d) -> p i d", d=D),
                              in_=x1_hbm[ch * 512:(ch + 1) * 512, :].rearrange("(i p) d -> p i d", p=128)),
                  inc=d_x1r[pp], k=16)
            x1r_chk2[ch] = d_x1r[pp].n

        def p3c_mixes(ch):
            pp = ch % 2
            ph.wait("vector", d_tr3[pp], tr_chk[ch])
            ph.wait("vector", d_wc, consts_loaded)
            if ch >= 2:
                ph.wait("vector", s_t, rk_done[ch - 2])
            for c in range(CT):
                dm = dmix_sb[:, (c % 2) * 512:(c % 2) * 512 + 512]
                xn_sl = xnT_sb[:, (pp * CT + c) * 512:(pp * CT + c + 1) * 512]
                seam = (zcol_sb[:] if ch == 0 else
                        xnT_sb[:, ((1 - pp) * CT + c) * 512 + 511:((1 - pp) * CT + c) * 512 + 512])
                ph.op("vector", lambda e, a=xn_sl, dm=dm: e.tensor_sub(dm[:, 1:512], a[:, 1:512], a[:, 0:511]))
                ph.op("vector", lambda e, a=xn_sl, sm=seam, dm=dm: e.tensor_sub(dm[:, 0:1], a[:, 0:1], sm))
                ph.drain("vector")
                xro = (pp * CT + c) * 512
                ph.op("vector", lambda e, c=c, a=xn_sl, xro=xro, dm=dm:
                      e.scalar_tensor_tensor(xrT_sb[:, xro + 1:xro + 512], dm[:, 1:512],
                                             mur_sb[:, c:c + 1], a[:, 0:511], ALU.mult, ALU.add))
                ph.op("vector", lambda e, c=c, sm=seam, xro=xro, dm=dm:
                      e.scalar_tensor_tensor(xrT_sb[:, xro:xro + 1], dm[:, 0:1],
                                             mur_sb[:, c:c + 1], sm, ALU.mult, ALU.add))
                ph.op("vector", lambda e, c=c, a=xn_sl, xro=xro, dm=dm:
                      e.scalar_tensor_tensor(xkT_sb[:, xro + 1:xro + 512], dm[:, 1:512],
                                             muk_sb[:, c:c + 1], a[:, 0:511], ALU.mult, ALU.add))
                ph.op("vector", lambda e, c=c, sm=seam, xro=xro, dm=dm:
                      e.scalar_tensor_tensor(xkT_sb[:, xro:xro + 1], dm[:, 0:1],
                                             muk_sb[:, c:c + 1], sm, ALU.mult, ALU.add),
                      inc=(s_v if c == CT - 1 else None))
            mix_done[ch] = s_v.n

        def p3c_rproj(ch):
            pp = ch % 2
            for it in range(4):
                m = pwo_use[0]
                ppw = m % 3
                ph.wait("tensor", s_v, mix_done[ch])
                ph.wait("tensor", d_wf, w_p3c_loaded)
                if m >= 3:
                    fsem, fcnt = pwo_free[m - 3]
                    ph.wait("tensor", fsem, fcnt)
                for ec in range(EC):
                    for c in range(CT):
                        ph.op("tensor", lambda e, ppw=ppw, c=c, it=it, ec=ec, pp=pp,
                              st=(c == 0), sp=(c == CT - 1):
                              e.matmul(ps_wo[ppw][:, ec * 512:(ec + 1) * 512],
                                       xrT_sb[:, (pp * CT + c) * 512 + it * 128:
                                              (pp * CT + c) * 512 + it * 128 + 128],
                                       wrt_sb[:, c * D + ec * 512: c * D + ec * 512 + 512],
                                       start=st, stop=sp),
                              inc=(s_t if (c == CT - 1 and ec == EC - 1) else None))
                r_chk = s_t.n
                ph.wait("scalar", s_t, r_chk)
                if ch >= 2:
                    ph.wait("scalar", s_gm, (ch - 2) * 4 + it + 1)
                ph.op("scalar", lambda e, it=it, ppw=ppw, pp=pp:
                      e.activation(rt_sb[:, (pp * 4 + it) * D:(pp * 4 + it + 1) * D],
                                   ps_wo[ppw][:, 0:D], AF.Tanh, scale=0.5), inc=s_s)
                rtanh_chk[ch * 4 + it] = s_s.n
                pwo_free[m] = (s_s, rtanh_chk[ch * 4 + it])
                pwo_use[0] += 1

        def p3c_kproj(ch):
            pp = ch % 2
            for o in range(CT):
                fo = ch * CT + o
                ppk = o % 2
                ph.wait("tensor", s_v, mix_done[ch])
                if fo >= 2:
                    ph.wait("tensor", s_s, krelu_chk[fo - 2])
                for c in range(CT):
                    ph.op("tensor", lambda e, ppk=ppk, c=c, o=o, pp=pp,
                          st=(c == 0), sp=(c == CT - 1):
                          e.matmul(ps_k3[ppk][:],
                                   wkt_sb[:, c * D + o * 128: c * D + o * 128 + 128],
                                   xkT_sb[:, (pp * CT + c) * 512:(pp * CT + c + 1) * 512],
                                   start=st, stop=sp),
                          inc=(s_t if c == CT - 1 else None))
                k_chk = s_t.n
                ph.wait("scalar", s_t, k_chk)
                if fo >= 2:
                    ph.wait("scalar", s_v, ksq_chk[fo - 2])
                ph.op("scalar", lambda e, ppk=ppk:
                      e.activation(k1_sb[:, ppk * 512:(ppk + 1) * 512], ps_k3[ppk][:], AF.Relu),
                      inc=s_s)
                krelu_chk[fo] = s_s.n
                ph.wait("vector", s_s, krelu_chk[fo])
                ph.op("vector", lambda e, o=o, ppk=ppk:
                      e.tensor_mul(kact_sb[:, o * 512:(o + 1) * 512],
                                   k1_sb[:, ppk * 512:(ppk + 1) * 512],
                                   k1_sb[:, ppk * 512:(ppk + 1) * 512]), inc=s_v)
                ksq_chk[fo] = s_v.n
            rk_done[ch] = s_t.n

        def p3c_vout(ch):
            kd = ksq_chk[ch * CT + CT - 1]
            for it in range(4):
                m = pwo_use[0]
                ppw = m % 3
                ph.wait("tensor", s_v, kd)
                if m >= 3:
                    fsem, fcnt = pwo_free[m - 3]
                    ph.wait("tensor", fsem, fcnt)
                for ec in range(EC):
                    for c in range(CT):
                        ph.op("tensor", lambda e, ppw=ppw, c=c, it=it, ec=ec,
                              st=(c == 0), sp=(c == CT - 1):
                              e.matmul(ps_wo[ppw][:, ec * 512:(ec + 1) * 512],
                                       kact_sb[:, c * 512 + it * 128: c * 512 + it * 128 + 128],
                                       wvt_sb[:, c * D + ec * 512: c * D + ec * 512 + 512],
                                       start=st, stop=sp),
                              inc=(s_t if (c == CT - 1 and ec == EC - 1) else None))
                v3_chk[ch * 4 + it] = s_t.n
                pwo_free[m] = (s_gm, ch * 4 + it + 1)
                pwo_bank[("v", ch, it)] = ppw
                pwo_use[0] += 1

        def p3c_gates(ch):
            pp = ch % 2
            ob_chks = {}

            def GA(it):
                ppw = it % 2
                ph.wait("scalar", s_s, rtanh_chk[ch * 4 + it])
                if ch * 4 + it >= 1:
                    ph.wait("scalar", s_gm, ch * 4 + it)
                ph.op("scalar", lambda e, it=it:
                      e.activation(g1_sb[:], rt_sb[:, (pp * 4 + it) * D:(pp * 4 + it + 1) * D],
                                   AF.Identity, scale=0.5, bias=ch05_sb[:]), inc=s_s)
                g1_chk = s_s.n
                ph.wait("vector", s_t, v3_chk[ch * 4 + it])
                ph.wait("vector", s_s, g1_chk)
                if it >= 2:
                    ph.wait("vector", s_s, ob_chks[it - 2])
                vb = pwo_bank[("v", ch, it)]
                gs = gg_sb[:, (it % 2) * D:(it % 2) * D + D]
                ph.op("vector", lambda e, vb=vb, gs=gs:
                      e.tensor_tensor(gs, g1_sb[:], ps_wo[vb][:, 0:D], ALU.mult),
                      inc=s_gm)
                ph.drain("vector")
                for half in range(EC):
                    ph.op("vector", lambda e, half=half, ppw=ppw, gs=gs:
                          e.bn_stats(st3_sb[ppw][:, half * 6:half * 6 + 6],
                                     gs[:, half * 512:(half + 1) * 512]))
                ph.drain("vector")
                ph.op("vector", lambda e, ppw=ppw:
                      e.bn_aggr(mv3_sb[ppw][:, 0:2], st3_sb[ppw][:, 0:6 * EC]), inc=s_v)

            def GB(it):
                ppw = it % 2
                ph.wait("scalar", s_v)
                ph.op("scalar", lambda e, ppw=ppw:
                      e.activation(rs3_sb[ppw][:, 2:3], mv3_sb[ppw][:, 1:2], AF.Ln, bias=ceps_sb[:]))
                ph.drain("scalar")
                ph.op("scalar", lambda e, ppw=ppw:
                      e.activation(rs3_sb[ppw][:, 2:3], rs3_sb[ppw][:, 2:3], AF.Exp, scale=-0.5))
                ph.drain("scalar")
                ph.op("scalar", lambda e, ppw=ppw, mc=(it % 2):
                      e.activation(ms_sb[:, mc:mc + 1], mv3_sb[ppw][:, 0:1], AF.Copy,
                                   scale=rs3_sb[ppw][:, 2:3]))
                ph.drain("scalar")
                ph.op("scalar", lambda e, mc=(it % 2):
                      e.activation(nms_sb[:, mc:mc + 1], ms_sb[:, mc:mc + 1], AF.Copy, scale=-1.0))
                ph.drain("scalar")
                off = pp * 4 * D + it * D
                if it == 0 and ch >= 2:
                    ph.wait("scalar", d_out[pp], out_w_chk[ch - 2])
                gs = gg_sb[:, (it % 2) * D:(it % 2) * D + D]
                ph.op("scalar", lambda e, off=off, ppw=ppw, gs=gs, mc=(it % 2):
                      e.activation(ob_sb[:, off:off + D], gs, AF.Identity,
                                   scale=rs3_sb[ppw][:, 2:3], bias=nms_sb[:, mc:mc + 1]),
                      inc=s_s)
                ob_chks[it] = s_s.n

            def GC(it):
                off = pp * 4 * D + it * D
                ph.wait("vector", s_s, ob_chks[it])
                ph.wait("vector", d_x1r[pp], x1r_chk2[ch])
                ph.op("vector", lambda e, off=off, it=it:
                      e.tensor_tensor(ob_sb[:, off:off + D], ob_sb[:, off:off + D],
                                      x1c_sb[:, it * D:(it + 1) * D], ALU.add), inc=s_v)

            for st_i in range(6):
                if st_i < 4:
                    GA(st_i)
                if 1 <= st_i < 5:
                    GB(st_i - 1)
                if st_i >= 2:
                    GC(st_i - 2)
            out_done[ch] = s_v.n
            ph.wait("sync", s_v, out_done[ch])
            ph.op("sync", lambda e, pp=pp, ch=ch:
                  e.dma_start(out=out_h[ch * 512:(ch + 1) * 512, :].rearrange("(i p) d -> p i d", p=128),
                              in_=ob_sb[:, pp * 4 * D:(pp + 1) * 4 * D].rearrange("p (i d) -> p i d", d=D)),
                  inc=d_out[pp], k=16)
            out_w_chk[ch] = d_out[pp].n

        for ch in range(QC):
            p3c_trloads(ch)
            p3c_mixes(ch)
            if ch >= 1:
                p3c_gates(ch - 1)
            p3c_x1load(ch)
            p3c_rproj(ch)
            p3c_kproj(ch)
            p3c_vout(ch)
        p3c_gates(QC - 1)
        ph.wait("sync", d_out[0], d_out[0].n)
        ph.wait("sync", d_out[1], d_out[1].n)

        ph.replay(block)
        p3c.close()
        p3sh.close()
    return nc


def _prep_inputs(L0_attn_latents, L0_attn_ln_gamma, L0_attn_ln_beta, L0_attn_Wq,
                 L0_attn_Wk, L0_attn_Wv, L0_attn_Wo, L0_ffn_W_r, L0_ffn_W_k,
                 L0_ffn_W_v, L0_ffn_mu_r, L0_ffn_mu_k):
    D = np.asarray(L0_attn_Wq).shape[0]
    CT = D // 128
    f = lambda a: np.ascontiguousarray(np.asarray(a, dtype=np.float32))
    com = {
        "wq": f(L0_attn_Wq), "wk": f(L0_attn_Wk), "wv": f(L0_attn_Wv), "wo": f(L0_attn_Wo),
        "wrt": f(np.asarray(L0_ffn_W_r).T), "wkt": f(np.asarray(L0_ffn_W_k).T),
        "wvt": f(np.asarray(L0_ffn_W_v).T),
        "gamma": f(np.asarray(L0_attn_ln_gamma).reshape(CT, 128).T),
        "beta": f(np.asarray(L0_attn_ln_beta).reshape(CT, 128).T),
        "mur": f(np.asarray(L0_ffn_mu_r).reshape(CT, 128).T),
        "muk": f(np.asarray(L0_ffn_mu_k).reshape(CT, 128).T),
    }
    x = f(L0_attn_latents)
    return x, com


def kernel(**inputs):
    x, com = _prep_inputs(**inputs)
    B, S, D = x.shape
    nc = build_nc(S=S, D=D, H=D // 64, dh=64)
    in_maps = [dict(com, x=np.ascontiguousarray(x[b])) for b in range(B)]
    res = run_bass_kernel_spmd(nc, in_maps, core_ids=list(range(B)))
    return np.stack([np.asarray(r["out"], dtype=np.float32) for r in res.results], axis=0)
